# version 42
# baseline (speedup 1.0000x reference)
import math
import numpy as np
import concourse.bass as bass
import concourse.mybir as mybir
from concourse.bass_utils import run_bass_kernel_spmd
from concourse.alu_op_type import AluOpType

F32 = mybir.dt.float32
BF16 = mybir.dt.bfloat16
AF = mybir.ActivationFunctionType
AX = mybir.AxisListType
OP = AluOpType

NCORES = 8
NSEQ = 2
S = 2048
D = 1024
H = 16
HD = 64
NP = 8
F4 = 4096
F4C = 32
NT = 16
TG = 2
NG = NT // TG
EPS_LN = 1e-5
EPS_Q = 1e-5
EPS_DEN = 1e-6
NW1 = F4 * D
NW2 = D * F4


class Sched:

    def __init__(self):
        self.ops = {k: [] for k in ("sync", "tensor", "vector", "scalar", "gpsimd")}

    def op(self, eng, fn, waits=(), inc=None, inc_n=1, self_wait=None):
        self.ops[eng].append((fn, tuple(waits), inc, inc_n, self_wait))

    def replay(self, name, e):
        seen = {}
        for fn, waits, inc, inc_n, self_wait in self.ops[name]:
            allw = waits if self_wait is None else (waits + (self_wait,))
            for sem, val in allw:
                if val <= 0 or seen.get(id(sem), -1) >= val:
                    continue
                e.wait_ge(sem.h, val)
                seen[id(sem)] = val
            ins = fn(e)
            if inc is not None:
                ins.then_inc(inc.h, inc_n)


class Sem:
    def __init__(self, h):
        self.h = h
        self.n = 0


def build_program(p2_groups=NG):
    nc = bass.Bass("TRN2", target_bir_lowering=False)

    xfm = nc.dram_tensor("xfm", [NSEQ, NP, 128, S], F32, kind="ExternalInput").ap()
    xtm = nc.dram_tensor("xtm", [NSEQ, NT, 128, D], F32, kind="ExternalInput").ap()
    wqT = nc.dram_tensor("wqT", [NP, 128, D], F32, kind="ExternalInput").ap()
    wkT = nc.dram_tensor("wkT", [NP, 128, D], F32, kind="ExternalInput").ap()
    wvT = nc.dram_tensor("wvT", [NP, 128, D], F32, kind="ExternalInput").ap()
    woT = nc.dram_tensor("woT", [NP, 128, D], F32, kind="ExternalInput").ap()
    w1r = nc.dram_tensor("w1r", [F4C, 128, NP, 128], F32, kind="ExternalInput").ap()
    w2r = nc.dram_tensor("w2r", [F4C, 128, D], F32, kind="ExternalInput").ap()
    bqd = nc.dram_tensor("bqd", [D], F32, kind="ExternalInput").ap()
    bkd = nc.dram_tensor("bkd", [D], F32, kind="ExternalInput").ap()
    bvd = nc.dram_tensor("bvd", [D], F32, kind="ExternalInput").ap()
    bod = nc.dram_tensor("bod", [D], F32, kind="ExternalInput").ap()
    b1d = nc.dram_tensor("b1d", [128, F4C], F32, kind="ExternalInput").ap()
    b2d = nc.dram_tensor("b2d", [D], F32, kind="ExternalInput").ap()
    out = nc.dram_tensor("out", [NSEQ, NT, 128, D], F32, kind="ExternalOutput").ap()
    w1q = nc.dram_tensor("w1q", [F4C, 128, NP, 128], BF16, kind="Internal").ap()
    w2q = nc.dram_tensor("w2q", [F4C, 128, D], BF16, kind="Internal").ap()
    ksum_hbm = nc.dram_tensor("ksum_hbm", [NSEQ, D], F32, kind="Internal").ap()
    gam_hbm = nc.dram_tensor("gam_hbm", [2, 1], F32, kind="Internal").ap()

    sch = Sched()
    alloc = []

    def sb(name, shape, dt):
        g = nc.sbuf_tensor(name, shape, dt)
        t = g.__enter__()
        alloc.append(g)
        return t

    wq_h = sb("wq_h", [128, NP, D], BF16)
    wq_l = sb("wq_l", [128, NP, D], BF16)
    wo_h = sb("wo_h", [128, NP, D], BF16)
    wo_l = sb("wo_l", [128, NP, D], BF16)
    xg = [sb("xg0", [128, NP, 128], F32), sb("xg1", [128, NP, 128], F32)]
    xgh = [sb(f"xgh{i}", [128, NP, 128], BF16) for i in range(4)]
    xgl = [sb(f"xgl{i}", [128, NP, 128], BF16) for i in range(4)]
    kvacc = [sb("kv0", [128, 8 * 129], F32), sb("kv1", [128, 8 * 129], F32)]
    ident = sb("ident", [128, 128], F32)
    ksrow = sb("ksrow", [8, 128], F32)
    b1_c = sb("b1_c", [128, F4C], F32)
    gsum2 = sb("gsum2", [2, 1], F32)

    p1_guards = []

    def sb1(name, shape, dt):
        g = nc.sbuf_tensor(name, shape, dt)
        t = g.__enter__()
        p1_guards.append(g)
        return t

    wk_h = sb1("wk_h", [128, NP, D], BF16)
    wk_l = sb1("wk_l", [128, NP, D], BF16)
    wv_h = sb1("wv_h", [128, NP, D], BF16)
    wv_l = sb1("wv_l", [128, NP, D], BF16)
    wrc = [sb1("wrc0", [128, D], F32), sb1("wrc1", [128, D], F32)]
    bk_r = sb1("bk_r", [128, D], F32)
    bv_r = sb1("bv_r", [128, D], F32)
    kkern = [sb1("kkern0", [128, D], F32), sb1("kkern1", [128, D], F32)]
    vprime = [sb1("vprime0", [128, 8 * 129], F32),
              sb1("vprime1", [128, 8 * 129], F32)]
    scr0 = sb1("scr0", [128, D], BF16)
    nm1 = sb1("nm1", [128, H], F32)
    wsl = [sb1("wsl0", [128, NP * 128], F32), sb1("wsl1", [128, NP * 128], F32)]
    wslq = [sb1("wslq0", [128, NP * 128], BF16),
            sb1("wslq1", [128, NP * 128], BF16)]
    absps = sb1("absps", [128, 72], F32)

    def ps(name):
        g = nc.psum_tensor(name, [128, 512], F32)
        t = g.__enter__()
        alloc.append(g)
        return t

    PB = [ps(f"p{i}") for i in range(8)]
    p0, p1, p2, p3, p4, p5, p6, p7 = PB

    def mksem(name):
        g = nc.semaphore(name)
        h = g.__enter__()
        alloc.append(g)
        return Sem(h)

    sT = mksem("sT")
    sV = mksem("sV")
    sS = mksem("sS")
    sG = mksem("sG")
    sB = mksem("sB")
    sXG = [mksem("sXG0"), mksem("sXG1")]
    sXT = [mksem("sXT0"), mksem("sXT1")]
    sWRC = [mksem("sWRC0"), mksem("sWRC1")]
    sWSL = [mksem("sWSL0"), mksem("sWSL1")]
    sWG = [mksem("sWG0"), mksem("sWG1")]
    sWP1 = [mksem(f"sWP1{i}") for i in range(3)]
    sWP2 = [mksem(f"sWP2{i}") for i in range(3)]
    sW1 = [mksem("sW10"), mksem("sW11"), mksem("sW12")]
    sW2 = [mksem("sW20"), mksem("sW21"), mksem("sW22")]
    sKS = mksem("sKS")
    sOT = [mksem("sOT0"), mksem("sOT1")]
    sQW = [mksem("sQW0"), mksem("sQW1")]
    sKSO = mksem("sKSO")
    sGAM = mksem("sGAM")

    def dma_in(sem, fn, waits=()):
        sch.op("sync", fn, waits=waits, inc=sem, inc_n=16)
        sem.n += 16
        return (sem, sem.n)

    def dma_out(sem, fn, waits=()):
        sch.op("gpsimd", fn, waits=waits, inc=sem, inc_n=16,
               self_wait=(sG, sG.n))
        sem.n += 16
        return (sem, sem.n)

    ENG_SEM = {"vector": sV, "scalar": sS, "gpsimd": sG}

    def ev(eng, sem, fn, waits=()):
        es = ENG_SEM.get(eng)
        if es is not None:
            assert sem is es, f"{eng} ops must inc their engine sem"
            sw = (es, es.n)
            sch.op(eng, fn, waits=waits, inc=sem, inc_n=1, self_wait=sw)
        else:
            sch.op(eng, fn, waits=waits, inc=sem, inc_n=1)
        sem.n += 1
        return sem.n

    def silent(eng, fn, waits=()):
        es = ENG_SEM.get(eng)
        if es is not None:
            sw = (es, es.n)
            sch.op(eng, fn, waits=waits, inc=es, inc_n=1, self_wait=sw)
            es.n += 1
        else:
            sch.op(eng, fn, waits=waits)

    eB = (sB, 0)
    for bsrc, bdst in ((bkd, bk_r), (bvd, bv_r)):
        eB = dma_in(sB, lambda e, s=bsrc, d=bdst: e.dma_start(
            out=d[:], in_=s[None, :].broadcast_to([128, D])), waits=(eB,))
    e_bias1 = eB
    eB = dma_in(sB, lambda e: e.dma_start(out=b1_c[:], in_=b1d[:, :]),
                waits=(eB,))
    e_b1c = eB

    silent("gpsimd", lambda e: e.memset(ident[:], 0.0))
    silent("gpsimd", lambda e: e.affine_select(
        out=ident[:], in_=ident[:], compare_op=OP.not_equal, fill=1.0,
        base=0, pattern=[[-1, 128]], channel_multiplier=1))
    silent("gpsimd", lambda e: e.memset(
        vprime[0][:].rearrange("p (a b) -> p a b", b=129)[:, :, 128:129], 1.0))
    e_gconst = ev("gpsimd", sG, lambda e: e.memset(
        vprime[1][:].rearrange("p (a b) -> p a b", b=129)[:, :, 128:129], 1.0))

    e_wrc_v = [0, 0]
    qslab_v = [0, 0]
    e_wrc_s = [0, 0]
    wg_state = {"k": 0}

    def split_chunk_sync(src, dsth, dstl, c):
        cb = c % 2
        e_d = dma_in(sWRC[cb], lambda e, src=src, c=c, cb=cb: e.dma_start(
            out=wrc[cb][:], in_=src[c, :, :]),
            waits=((sV, e_wrc_v[cb]), (sS, e_wrc_s[cb])))
        e_wrc_s[cb] = ev("scalar", sS, lambda e, c=c, cb=cb, dsth=dsth:
                         e.activation(
            dsth[:, c, :], wrc[cb][:], AF.Identity), waits=(e_d,))
        e_l = ev("vector", sV, lambda e, c=c, cb=cb, dsth=dsth, dstl=dstl:
                 e.scalar_tensor_tensor(
            dstl[:, c, :], wrc[cb][:], 1.0, dsth[:, c, :],
            op0=OP.mult, op1=OP.subtract), waits=((sS, e_wrc_s[cb]),))
        e_wrc_v[cb] = e_l
        return e_l

    def split_chunk_g(src, dsth, dstl, c):
        b = wg_state["k"] % 2
        wg_state["k"] += 1
        e_d = dma_out(sWG[b], lambda e, src=src, c=c, b=b: e.dma_start(
            out=wsl[b][:], in_=src[c, :, :]), waits=((sV, qslab_v[b]),))
        e_h = ev("scalar", sS, lambda e, c=c, b=b, dsth=dsth: e.activation(
            dsth[:, c, :], wsl[b][:], AF.Identity), waits=(e_d,))
        e_l = ev("vector", sV, lambda e, c=c, b=b, dsth=dsth, dstl=dstl:
                 e.scalar_tensor_tensor(
            dstl[:, c, :], wsl[b][:], 1.0, dsth[:, c, :],
            op0=OP.mult, op1=OP.subtract), waits=((sS, e_h),))
        qslab_v[b] = e_l
        return e_l

    xg_state = {"dma": [(sXG[0], 0), (sXG[1], 0)], "split": [0, 0, 0, 0],
                "splitb2": [0, 0], "tfree": [0, 0, 0, 0]}

    def load_x_tile(si, i):
        bp = i % 2
        p4 = i % 4
        t0 = i * 128
        e_d = dma_in(sXG[bp], lambda e, si=si, t0=t0, bp=bp: e.dma_start(
            out=xg[bp][:],
            in_=xfm[si, :, :, t0:t0 + 128].rearrange("c p t -> p c t")),
            waits=((sV, xg_state["splitb2"][bp]),))
        e_h = ev("scalar", sS, lambda e, bp=bp, p4=p4: e.activation(
            xgh[p4][:].rearrange("p c t -> p (c t)"),
            xg[bp][:].rearrange("p c t -> p (c t)"), AF.Identity),
            waits=(e_d, (sT, xg_state["tfree"][p4])))
        e_s = ev("vector", sV, lambda e, bp=bp, p4=p4: e.scalar_tensor_tensor(
            xgl[p4][:].rearrange("p c t -> p (c t)"),
            xg[bp][:].rearrange("p c t -> p (c t)"), 1.0,
            xgh[p4][:].rearrange("p c t -> p (c t)"),
            op0=OP.mult, op1=OP.subtract), waits=((sS, e_h),))
        xg_state["dma"][bp] = e_d
        xg_state["split"][p4] = e_s
        xg_state["splitb2"][bp] = e_s
        return e_s

    def proj3(bp, wh, wl, pa, pb_, first_waits, set_tfree=False,
              per_chunk=None):
        last = 0
        for c in range(NP):
            w = first_waits if c == 0 else ()
            if per_chunk is not None:
                w = w + ((sV, per_chunk[c]),)
            st = (c == 0)
            sp = (c == NP - 1)
            silent("tensor", lambda e, c=c, st=st, bp=bp, wh=wh, pa=pa: e.matmul(
                pa[:, 0:512], xgh[bp][:, c, :], wh[:, c, 0:512],
                start=st, stop=False), waits=w)
            silent("tensor", lambda e, c=c, st=st, bp=bp, wh=wh, pb_=pb_: e.matmul(
                pb_[:, 0:512], xgh[bp][:, c, :], wh[:, c, 512:1024],
                start=st, stop=False))
            silent("tensor", lambda e, c=c, bp=bp, wl=wl, pa=pa: e.matmul(
                pa[:, 0:512], xgh[bp][:, c, :], wl[:, c, 0:512],
                start=False, stop=False))
            silent("tensor", lambda e, c=c, bp=bp, wl=wl, pb_=pb_: e.matmul(
                pb_[:, 0:512], xgh[bp][:, c, :], wl[:, c, 512:1024],
                start=False, stop=False))
            silent("tensor", lambda e, c=c, sp=sp, bp=bp, wh=wh, pa=pa: e.matmul(
                pa[:, 0:512], xgl[bp][:, c, :], wh[:, c, 0:512],
                start=False, stop=sp))
            if c == NP - 1:
                last = ev("tensor", sT, lambda e, c=c, bp=bp, wh=wh, pb_=pb_:
                          e.matmul(pb_[:, 0:512], xgl[bp][:, c, :],
                                   wh[:, c, 512:1024], start=False, stop=True))
            else:
                silent("tensor", lambda e, c=c, bp=bp, wh=wh, pb_=pb_: e.matmul(
                    pb_[:, 0:512], xgl[bp][:, c, :], wh[:, c, 512:1024],
                    start=False, stop=False))
        if set_tfree:
            xg_state["tfree"][bp] = last
        return last

    qst = {"gi": 0, "qi": 0, "e_thr": 0, "e_gam_p7": 0, "g_used": 0,
           "slab_s": [0, 0],
           "slabq_out": [(sQW[0], 0), (sQW[1], 0)]}

    def q_src(i):
        if i < 32:
            return w1r[i, :, :, :].rearrange("p c d -> p (c d)")
        return w2r[i - 32, :, :]

    def emit_gamma_slab():
        i = qst["gi"]
        if i >= 64:
            return
        qst["gi"] += 1
        b = i % 2
        e_d = dma_in(sWSL[b], lambda e, i=i, b=b: e.dma_start(
            out=wsl[b][:], in_=q_src(i)), waits=((sV, qslab_v[b]),))
        qslab_v[b] = ev("vector", sV, lambda e, i=i, b=b: e.tensor_reduce(
            absps[:, i:i + 1], wsl[b][:], axis=AX.X, op=OP.add,
            apply_absolute_value=True), waits=(e_d,))

    def emit_gamma_finalize():
        nonlocal eB
        ev("vector", sV, lambda e: e.tensor_reduce(
            absps[:, 64:65], absps[:, 0:32], axis=AX.X, op=OP.add),
            waits=((sV, max(qslab_v)),))
        e2 = ev("vector", sV, lambda e: e.tensor_reduce(
            absps[:, 65:66], absps[:, 32:64], axis=AX.X, op=OP.add))
        e_t = ev("tensor", sT, lambda e: e.transpose(
            p7[0:2, 0:128], absps[:, 64:66], ident[:]),
            waits=((sV, e2), (sG, e_gconst)))
        e3 = ev("vector", sV, lambda e: e.tensor_reduce(
            gsum2[:, 0:1], p7[0:2, 0:128], axis=AX.X, op=OP.add),
            waits=((sT, e_t),))
        qst["e_gam_p7"] = e3
        e_g = dma_out(sGAM, lambda e: e.dma_start(
            out=gam_hbm[:, :], in_=gsum2[:, :]), waits=((sV, e3),))
        eB = dma_in(sB, lambda e: e.dma_start(
            out=absps[:, 66:67],
            in_=gam_hbm[0, :][None, :].broadcast_to([128, 1])),
            waits=(eB, e_g))
        eB = dma_in(sB, lambda e: e.dma_start(
            out=absps[:, 67:68],
            in_=gam_hbm[1, :][None, :].broadcast_to([128, 1])), waits=(eB,))
        silent("vector", lambda e: e.tensor_scalar(
            absps[:, 68:69], absps[:, 66:67], 0.5 / NW1, 0.5 * EPS_Q,
            op0=OP.mult, op1=OP.add), waits=(eB,))
        silent("vector", lambda e: e.tensor_scalar(
            absps[:, 69:70], absps[:, 68:69], -1.0, None, op0=OP.mult))
        silent("vector", lambda e: e.tensor_scalar(
            absps[:, 70:71], absps[:, 67:68], 0.5 / NW2, 0.5 * EPS_Q,
            op0=OP.mult, op1=OP.add))
        qst["e_thr"] = ev("vector", sV, lambda e: e.tensor_scalar(
            absps[:, 71:72], absps[:, 70:71], -1.0, None, op0=OP.mult))

    def emit_quant_slab():
        i = qst["qi"]
        if i >= 64:
            return
        qst["qi"] += 1
        b = i % 2
        if i < 32:
            dst = w1q[i, :, :, :].rearrange("p c d -> p (c d)")
            tpos, tneg = 68, 69
        else:
            dst = w2q[i - 32, :, :]
            tpos, tneg = 70, 71
        e_d = dma_in(sWSL[b], lambda e, i=i, b=b: e.dma_start(
            out=wsl[b][:], in_=q_src(i)), waits=((sV, qslab_v[b]),))
        silent("vector", lambda e, b=b, tneg=tneg: e.tensor_scalar(
            scr0[:], wsl[b][:], absps[:, tneg:tneg + 1], None, op0=OP.is_lt),
            waits=(e_d, (sV, qst["e_thr"]), qst["slabq_out"][b]))
        qslab_v[b] = ev("vector", sV, lambda e, b=b, tpos=tpos:
                        e.scalar_tensor_tensor(
            wslq[b][:], wsl[b][:], absps[:, tpos:tpos + 1], scr0[:],
            op0=OP.is_gt, op1=OP.subtract))
        qst["slabq_out"][b] = dma_out(
            sQW[b], lambda e, b=b, d=dst: e.dma_start(out=d, in_=wslq[b][:]),
            waits=((sV, qslab_v[b]),))

    ev_kevac = 0
    ev_vevac = 0
    ev_kk = 0
    ev_kvevac = 0
    e_kvmm = 0
    e_kvmm_kb = [0, 0]
    e_ksum_out = [(sKSO, 0), (sKSO, 0)]
    e_kvz = [0, 0]
    e_kr_last = 0

    load_x_tile(0, 0)
    wk_evs = []
    wv_evs = []
    for c in range(NP):
        wk_evs.append(split_chunk_sync(wkT, wk_h, wk_l, c))
        wv_evs.append(split_chunk_g(wvT, wv_h, wv_l, c))
    e_wkv = wk_evs[-1]
    wqo_pend = [(wqT, wq_h, wq_l, c) for c in range(NP)] + \
               [(woT, wo_h, wo_l, c) for c in range(NP)]
    pend_kv = None

    def emit_kv(i, kb, e_kk, e_vev):
        nonlocal e_kvmm
        for pr in range(8):
            bank = PB[4 + pr // 3]
            off = (pr % 3) * 129
            w = (((sS, e_kk), (sV, max(e_vev, ev_kvevac)))
                 if pr == 0 else ())
            fn = (lambda e, pr=pr, bank=bank, off=off, i=i, kb=kb: e.matmul(
                bank[:, off:off + 129],
                kkern[kb][:, pr * 128:(pr + 1) * 128],
                vprime[kb][:, pr * 129:(pr + 1) * 129],
                start=(i == 0 and pr % 3 == 0),
                stop=(i == NT - 1 and (pr % 3 == 2 or pr == 7))))
            if pr == 7:
                e_kvmm = ev("tensor", sT, fn, waits=w)
                e_kvmm_kb[kb] = e_kvmm
            else:
                silent("tensor", fn, waits=w)

    def emit_seq_boundary(si):
        nonlocal ev_kvevac, e_kr_last
        kva = kvacc[si]
        silent("vector", lambda e, kva=kva: e.tensor_copy(
            kva[:, 0:387], p4[:, 0:387]), waits=((sT, e_kvmm),))
        silent("vector", lambda e, kva=kva: e.tensor_copy(
            kva[:, 387:774], p5[:, 0:387]))
        ev_kvevac = ev("vector", sV, lambda e, kva=kva: e.tensor_copy(
            kva[:, 774:1032], p6[:, 0:258]))
        for pr in range(8):
            col = pr * 129
            silent("gpsimd", lambda e, kva=kva, col=col: e.memset(
                kva[0:64, col + 64:col + 128], 0.0), waits=((sV, ev_kvevac),))
            if pr == 7:
                e_kvz[si] = ev("gpsimd", sG, lambda e, kva=kva, col=col:
                               e.memset(kva[64:128, col:col + 64], 0.0))
            else:
                silent("gpsimd", lambda e, kva=kva, col=col: e.memset(
                    kva[64:128, col:col + 64], 0.0))
        e_t = ev("tensor", sT, lambda e, kva=kva: e.transpose(
            p7[0:8, 0:128],
            kva[:].rearrange("p (a b) -> p a b", b=129)[:, :, 128],
            ident[:]), waits=((sV, max(ev_kvevac, qst["e_gam_p7"])),))
        e_kr_last = ev("vector", sV, lambda e: e.tensor_copy(
            ksrow[:, :], p7[0:8, 0:128]),
            waits=((sT, e_t), (sKSO, 16 * si)))
        e_ksum_out[si] = dma_out(sKSO, lambda e, si=si: e.dma_start(
            out=ksum_hbm[si, :].rearrange("(a b) -> a b", a=8),
            in_=ksrow[:, :]),
            waits=((sV, e_kr_last), (sKSO, 16 * si)))

    for si in range(NSEQ):
        for i in range(NT):
            bp = i % 4
            if i < NT - 1:
                load_x_tile(si, i + 1)
            elif si == 0:
                load_x_tile(1, 0)
            if si == 0:
                for _ in range(2):
                    if wqo_pend:
                        split_chunk_g(*wqo_pend.pop(0))
                for _ in range(4):
                    emit_gamma_slab()
            elif not (si == 1 and i == 0):
                for _ in range(5 if i in (1, 2) else 4):
                    emit_quant_slab()
            first_tile = (si == 0 and i == 0)
            e_kp = proj3(bp, wk_h, wk_l, p0, p1,
                         (e_bias1,
                          (sV, max(xg_state["split"][bp], ev_kevac, e_wkv))),
                         set_tfree=False,
                         per_chunk=wk_evs if first_tile else None)
            e_vpmm = proj3(bp, wv_h, wv_l, p2, p3, ((sV, ev_vevac),),
                           set_tfree=True,
                           per_chunk=wv_evs if first_tile else None)
            if pend_kv is not None:
                emit_kv(*pend_kv)
                pend_kv = None
            kb = i % 2
            silent("vector", lambda e, kb=kb: e.scalar_tensor_tensor(
                kkern[kb][:, 0:512], p0[:, 0:512], 1.0, bk_r[:, 0:512],
                op0=OP.mult, op1=OP.add),
                waits=((sT, max(e_kp, e_kvmm_kb[kb])),))
            ev_kevac = ev("vector", sV, lambda e, kb=kb: e.scalar_tensor_tensor(
                kkern[kb][:, 512:1024], p1[:, 0:512], 1.0, bk_r[:, 512:1024],
                op0=OP.mult, op1=OP.add))
            silent("vector", lambda e, kb=kb: e.tensor_reduce(
                nm1[:].rearrange("p (h o) -> p h o", o=1),
                kkern[kb][:].rearrange("p (h d) -> p h d", d=HD),
                axis=AX.X, op=OP.max, negate=True))
            e_sub = ev("vector", sV, lambda e, kb=kb: e.tensor_tensor(
                kkern[kb][:].rearrange("p (h d) -> p h d", d=HD),
                kkern[kb][:].rearrange("p (h d) -> p h d", d=HD),
                nm1[:].rearrange("p (h o) -> p h o", o=1).broadcast_to(
                    [128, H, HD]), op=OP.add))
            ev_kk = ev("scalar", sS, lambda e, kb=kb: e.activation(
                kkern[kb][:], kkern[kb][:], AF.Exp), waits=((sV, e_sub),))
            silent("vector", lambda e, kb=kb: e.scalar_tensor_tensor(
                vprime[kb][:].rearrange("p (a b) -> p a b", b=129)
                [:, 0:4, 0:128],
                p2[:, 0:512], 1.0, bv_r[:, 0:512], op0=OP.mult, op1=OP.add),
                waits=((sT, max(e_vpmm, e_kvmm_kb[kb])), (sG, e_gconst)))
            ev_vevac = ev("vector", sV, lambda e, kb=kb: e.scalar_tensor_tensor(
                vprime[kb][:].rearrange("p (a b) -> p a b", b=129)
                [:, 4:8, 0:128],
                p3[:, 0:512], 1.0, bv_r[:, 512:1024], op0=OP.mult, op1=OP.add))
            pend_kv = (i, kb, ev_kk, ev_vevac)
            if si == 0 and i == NT - 1:
                emit_gamma_finalize()
            if si == 1 and i == 0:
                emit_seq_boundary(0)
                for _ in range(2):
                    emit_quant_slab()
        if pend_kv is not None:
            emit_kv(*pend_kv)
            pend_kv = None

    while qst["qi"] < 64:
        emit_quant_slab()
    e_scr_all = (qst["slabq_out"][0], qst["slabq_out"][1])
    eT1 = sT.n
    eV1 = sV.n
    eS1 = sS.n
    eG1 = sG.n

    for g in reversed(p1_guards):
        g.__exit__(None, None, None)

    bq_r = sb("bq_r", [128, D], F32)
    bo_r = sb("bo_r", [128, D], F32)
    b2_r = sb("b2_r", [128, D], F32)
    ksrep = sb("ksrep", [128, D], F32)
    xt = [sb("xt0", [128, D], F32), sb("xt1", [128, D], F32)]
    qb = [sb("qb0", [128, D], F32), sb("qb1", [128, D], F32)]
    qs = [sb("qs0", [128, D], F32), sb("qs1", [128, D], F32)]
    qsT = [qb[0].rearrange("p (a b) -> p a b", b=128),
           qb[1].rearrange("p (a b) -> p a b", b=128)]
    dt = qb
    nm = [sb("nm0", [128, H], F32), sb("nm1b", [128, H], F32)]
    den = [sb("den0", [128, H], F32), sb("den1", [128, H], F32)]
    yTh = [sb("yTh0", [128, NP, 128], BF16), sb("yTh1", [128, NP, 128], BF16)]
    yTl = [sb("yTl0", [128, NP, 128], BF16), sb("yTl1", [128, NP, 128], BF16)]
    x1 = [sb("x10", [128, TG, D], F32), sb("x11", [128, TG, D], F32)]
    x1Th = [sb("x1Th0", [128, NP, TG * 128], BF16),
            sb("x1Th1", [128, NP, TG * 128], BF16)]
    x1Tl = [sb("x1Tl0", [128, NP, TG * 128], BF16),
            sb("x1Tl1", [128, NP, TG * 128], BF16)]
    w1s = [sb(f"w1s{i}", [128, NP, 128], BF16) for i in range(3)]
    w2s = [sb(f"w2s{i}", [128, D], BF16) for i in range(3)]
    hh = [sb("hh0", [128, TG * 128], BF16), sb("hh1", [128, TG * 128], BF16)]
    hl = [sb("hl0", [128, TG * 128], BF16), sb("hl1", [128, TG * 128], BF16)]
    h32 = [sb("h320", [128, TG * 128], F32), sb("h321", [128, TG * 128], F32)]
    lnt = sb("lnt", [128, 36], F32)
    ot2 = sb("ot2", [128, TG, D], F32)

    e_eps = ev("gpsimd", sG, lambda e: e.memset(lnt[:, 32:33], EPS_LN),
               waits=((sT, eT1), (sV, eV1), (sS, eS1)))

    st2 = {
        "r1_evac": [ev_kevac, ev_vevac],
        "qb_evac": [0, 0],
        "qb_free": [eS1, eS1],
        "ffnout_evac": 0,
        "h_evac": [0, 0],
        "h_hi": [0, 0],
        "w1_free": [eT1, eT1, eT1],
        "e_mm1_last": eT1,
        "x1T_ready": [0, 0],
        "x1T_rr": [[0, 0], [0, 0]],
        "x1T_hi_rr": [[0, 0], [0, 0]],
        "x1_ready": [0, 0, 0, 0],
        "x1T_free": eT1,
        "xt_rdy": [0, 0],
        "xt_free": [0, 0],
        "ot_free": [(sOT[0], 0), (sOT[1], 0)],
        "ksrep_dma": (sKS, 0),
        "ksrep_free": 0,
        "f4v": ev_kvevac, "f5v": ev_kvevac,
        "f4s": eS1, "f5s": eS1,
        "f6v": ev_kvevac, "f7v": e_kr_last,
        "qs_free": [eT1, eT1],
        "qs_sfree": [eS1, eS1],
        "qsc": [0, 0],
        "qc_exp": [0, 0],
        "w_pre": None,
        "last_ffn": False,
        "qsT_free": [eT1, eT1],
        "yT_free": [eT1, eT1],
        "mm2_ev": [eT1, eT1, eT1],
        "lnt1_free": [eS1, eS1],
        "lnt2_free": eS1,
        "r2_ev": [0, 0],
    }

    load_x_tile(0, 0)
    load_x_tile(0, 1)

    eB = dma_in(sB, lambda e: e.dma_start(
        out=bq_r[:], in_=bqd[None, :].broadcast_to([128, D])),
        waits=(eB, (sT, eT1), (sV, eV1), (sS, eS1), (sG, eG1),
               e_scr_all[0], e_scr_all[1], e_ksum_out[1], (sGAM, 16)))
    e_bias_q = eB
    eB = dma_in(sB, lambda e: e.dma_start(
        out=bo_r[:], in_=bod[None, :].broadcast_to([128, D])), waits=(eB,))
    e_bias_o = eB
    eB = dma_in(sB, lambda e: e.dma_start(
        out=b2_r[:], in_=b2d[None, :].broadcast_to([128, D])), waits=(eB,))
    e_bias2 = eB

    def emit_ln_red(buf3, ti, cb, e_prior):
        return ev("vector", sV, lambda e, buf3=buf3, ti=ti, cb=cb:
                  e.tensor_reduce(
            lnt[:, cb + ti:cb + ti + 1], buf3[:, ti, :],
            axis=AX.X, op=OP.add), waits=((sV, e_prior),))

    def emit_ln_seg1b(sq_scale, cb, e_lnt_free):
        silent("vector", lambda e, cb=cb: e.tensor_scalar(
            lnt[:, cb + 2:cb + 4], lnt[:, cb:cb + 2], 1.0 / D, None,
            op0=OP.mult))
        return ev("vector", sV, lambda e, cb=cb, sq_scale=sq_scale:
                  e.tensor_scalar(
            lnt[:, cb + 14:cb + 16], lnt[:, cb + 2:cb + 4], -sq_scale, None,
            op0=OP.mult), waits=((sS, e_lnt_free),))

    def emit_ln_seg2(buf3, ti, sq_scale, cb, dump, e_seg1, extra_waits):
        return ev("scalar", sS, lambda e, buf3=buf3, ti=ti, cb=cb,
                  sq_scale=sq_scale, dump=dump: e.activation(
            dump, buf3[:, ti, :], AF.Square, scale=sq_scale,
            bias=lnt[:, cb + 14 + ti:cb + 15 + ti],
            accum_out=lnt[:, cb + 4 + ti:cb + 5 + ti]),
            waits=((sV, e_seg1),) + extra_waits)

    def emit_ln_seg3(sq_scale, cb, e_seg2):
        e_v1 = ev("vector", sV, lambda e, cb=cb, sq_scale=sq_scale:
                  e.tensor_scalar(
            lnt[:, cb + 4:cb + 6], lnt[:, cb + 4:cb + 6],
            1.0 / (D * sq_scale * sq_scale), EPS_LN,
            op0=OP.mult, op1=OP.add), waits=((sS, e_seg2),))
        silent("scalar", lambda e, cb=cb: e.activation(
            lnt[:, cb + 6:cb + 8], lnt[:, cb + 4:cb + 6], AF.Ln),
            waits=((sV, e_v1),))
        e_ex = ev("scalar", sS, lambda e, cb=cb: e.activation(
            lnt[:, cb + 12:cb + 14], lnt[:, cb + 6:cb + 8], AF.Exp,
            scale=-0.5))
        return ev("vector", sV, lambda e, cb=cb: e.scalar_tensor_tensor(
            lnt[:, cb + 10:cb + 12], lnt[:, cb + 2:cb + 4], -1.0,
            lnt[:, cb + 12:cb + 14], op0=OP.mult, op1=OP.mult),
            waits=((sS, e_ex),))

    def load_xt(si, gi, ti):
        tb = ti % 2
        e_d = dma_in(sXT[tb], lambda e, si=si, gi=gi, ti=ti,
                     tb=tb: e.dma_start(
            out=xt[tb][:], in_=xtm[si, gi * TG + ti, :, :]),
            waits=((sV, st2["xt_free"][tb]),))
        st2["xt_rdy"][tb] = ev("vector", sV, lambda e, tb=tb: e.tensor_tensor(
            xt[tb][:], xt[tb][:], bo_r[:], op=OP.add),
            waits=(e_d, e_bias_o))

    qpb = [(p0, p1), (p2, p3)]

    def emit_qproj(si, gi):
        for ti in range(TG):
            bp = (gi * TG + ti) % 4
            pa, pb_ = qpb[ti]
            e_qp = proj3(bp, wq_h, wq_l, pa, pb_,
                         ((sV, max(xg_state["split"][bp],
                                   st2["r1_evac"][ti])),),
                         set_tfree=True)
            silent("vector", lambda e, ti=ti, pa=pa: e.scalar_tensor_tensor(
                qb[ti][:, 0:512], pa[:, 0:512], 1.0, bq_r[:, 0:512],
                op0=OP.mult, op1=OP.add), waits=((sT, e_qp), e_bias_q))
            st2["qb_evac"][ti] = ev("vector", sV, lambda e, ti=ti, pb_=pb_:
                                    e.scalar_tensor_tensor(
                qb[ti][:, 512:1024], pb_[:, 0:512], 1.0, bq_r[:, 512:1024],
                op0=OP.mult, op1=OP.add))
    def emit_qchain_tile(si, gi, c):
        if c in (2, 3, 4):
            ti, step = 0, c - 2
        elif c in (10, 11, 12):
            ti, step = 1, c - 10
        else:
            return
        if step == 0:
            ev("vector", sV, lambda e, ti=ti: e.tensor_reduce(
                nm[ti][:].rearrange("p (h o) -> p h o", o=1),
                qb[ti][:].rearrange("p (h d) -> p h d", d=HD),
                axis=AX.X, op=OP.max, negate=True),
                waits=((sV, st2["qb_evac"][ti]),))
            e_sub = ev("vector", sV, lambda e, ti=ti: e.tensor_tensor(
                qs[ti][:].rearrange("p (h d) -> p h d", d=HD),
                qb[ti][:].rearrange("p (h d) -> p h d", d=HD),
                nm[ti][:].rearrange("p (h o) -> p h o", o=1).broadcast_to(
                    [128, H, HD]), op=OP.add),
                waits=((sT, st2["qs_free"][ti]), (sS, st2["qs_sfree"][ti])))
            e_exp = ev("scalar", sS, lambda e, ti=ti: e.activation(
                qs[ti][:], qs[ti][:], AF.Exp), waits=((sV, e_sub),))
            st2["qs_sfree"][ti] = e_exp
            st2["qc_exp"][ti] = e_exp
        elif step == 1:
            silent("vector", lambda e, ti=ti: e.tensor_tensor(
                dt[ti][:], qs[ti][:], ksrep[:], op=OP.mult),
                waits=((sS, st2["qc_exp"][ti]), st2["ksrep_dma"]))
            silent("vector", lambda e, ti=ti: e.tensor_reduce(
                den[ti][:].rearrange("p (h o) -> p h o", o=1),
                dt[ti][:].rearrange("p (h d) -> p h d", d=HD),
                axis=AX.X, op=OP.add))
        else:
            silent("vector", lambda e, ti=ti: e.tensor_scalar(
                den[ti][:], den[ti][:], EPS_DEN, None, op0=OP.add))
            silent("vector", lambda e, ti=ti: e.reciprocal(
                den[ti][:], den[ti][:]))
            e_qsc = ev("vector", sV, lambda e, ti=ti: e.tensor_tensor(
                qs[ti][:].rearrange("p (h d) -> p h d", d=HD),
                qs[ti][:].rearrange("p (h d) -> p h d", d=HD),
                den[ti][:].rearrange("p (h o) -> p h o", o=1).broadcast_to(
                    [128, H, HD]), op=OP.mult))
            st2["ksrep_free"] = e_qsc
            st2["qsc"][ti] = e_qsc

    def emit_x1tr(si, gi, tis=(0, 1)):
        par = gi % 2
        x1buf = x1[par]
        for ti in tis:
            for rr in range(2):
                bankT = p6 if rr == 0 else p7
                e_tr = 0
                fvk = "f6v" if rr == 0 else "f7v"
                for j in range(4):
                    c = rr * 4 + j
                    w = ()
                    if j == 0:
                        w = ((sS, st2["x1_ready"][2 * par + ti]),
                             (sV, st2[fvk]))
                    fn = (lambda e, c=c, j=j, x1buf=x1buf, ti=ti, bankT=bankT:
                          e.transpose(bankT[:, j * 128:(j + 1) * 128],
                                      x1buf[:, ti, c * 128:(c + 1) * 128],
                                      ident[:]))
                    if j == 3:
                        e_tr = ev("tensor", sT, fn, waits=w)
                    else:
                        silent("tensor", fn, waits=w)
                e_hi = ev("scalar", sS, lambda e, rr=rr, par=par, ti=ti,
                          bankT=bankT: e.activation(
                    x1Th[par][:, rr * 4:(rr + 1) * 4,
                              ti * 128:(ti + 1) * 128],
                    bankT[:, 0:512], AF.Identity),
                    waits=((sT, max(e_tr, st2["x1T_free"])),))
                e_lo = ev("vector", sV, lambda e, rr=rr, par=par, ti=ti,
                          bankT=bankT: e.scalar_tensor_tensor(
                    x1Tl[par][:, rr * 4:(rr + 1) * 4,
                              ti * 128:(ti + 1) * 128],
                    bankT[:, 0:512], 1.0,
                    x1Th[par][:, rr * 4:(rr + 1) * 4,
                              ti * 128:(ti + 1) * 128],
                    op0=OP.mult, op1=OP.subtract), waits=((sS, e_hi),))
                st2[fvk] = e_lo
                if ti == TG - 1:
                    st2["x1T_rr"][par][rr] = e_lo
                    st2["x1T_hi_rr"][par][rr] = e_hi
                    if rr == 1:
                        st2["x1T_ready"][par] = e_lo

    def emit_ln1_tile(x1buf, ti, par, e_r1):
        sq = 0.5
        ev("vector", sV, lambda e, x1buf=x1buf, ti=ti: e.tensor_reduce(
            lnt[:, ti:ti + 1], x1buf[:, ti, :], axis=AX.X, op=OP.add),
            waits=((sV, e_r1),))
        silent("vector", lambda e, ti=ti: e.tensor_scalar(
            lnt[:, 2 + ti:3 + ti], lnt[:, ti:ti + 1], 1.0 / D, None,
            op0=OP.mult))
        silent("vector", lambda e, ti=ti: e.tensor_scalar(
            lnt[:, 8 + ti:9 + ti], lnt[:, 2 + ti:3 + ti], -1.0, None,
            op0=OP.mult), waits=((sS, st2["lnt1_free"][ti]),))
        e_nmb = ev("vector", sV, lambda e, ti=ti: e.tensor_scalar(
            lnt[:, 14 + ti:15 + ti], lnt[:, 2 + ti:3 + ti], -(D ** -0.5),
            None, op0=OP.mult))
        e_sq = ev("scalar", sS, lambda e, x1buf=x1buf, ti=ti:
                  e.activation(
            qs[ti][:], x1buf[:, ti, :], AF.Square, scale=D ** -0.5,
            bias=lnt[:, 14 + ti:15 + ti],
            accum_out=lnt[:, 4 + ti:5 + ti]),
            waits=((sV, e_nmb), (sT, st2["qs_free"][ti])))
        st2["qs_sfree"][ti] = e_sq
        silent("scalar", lambda e, ti=ti: e.activation(
            lnt[:, 6 + ti:7 + ti], lnt[:, 4 + ti:5 + ti], AF.Ln,
            bias=lnt[:, 32:33]), waits=((sG, e_eps),))
        silent("scalar", lambda e, ti=ti: e.activation(
            lnt[:, 12 + ti:13 + ti], lnt[:, 6 + ti:7 + ti], AF.Exp,
            scale=-0.5))
        silent("scalar", lambda e, ti=ti: e.activation(
            lnt[:, 10 + ti:11 + ti], lnt[:, 8 + ti:9 + ti], AF.Identity,
            scale=lnt[:, 12 + ti:13 + ti]))
        e_n = ev("scalar", sS, lambda e, x1buf=x1buf, ti=ti: e.activation(
            x1buf[:, ti, :], x1buf[:, ti, :], AF.Identity,
            bias=lnt[:, 10 + ti:11 + ti], scale=lnt[:, 12 + ti:13 + ti]))
        st2["lnt1_free"][ti] = e_n
        st2["x1_ready"][2 * par + ti] = e_n

    def emit_attn_rest(si, gi):
        par = gi % 2
        e_yv = [0, 0]
        for ti in range(TG):
            e_qsT_r = [0, 0]
            for rr in range(2):
                bankQ = p4 if rr == 0 else p5
                fv = "f4v" if rr == 0 else "f5v"
                fs = "f4s" if rr == 0 else "f5s"
                e_tr = 0
                for j in range(4):
                    c = rr * 4 + j
                    w = ()
                    if j == 0:
                        w = ((sV, max(st2["qsc"][ti], st2[fv])),
                             (sS, st2[fs]))
                    fn = (lambda e, c=c, j=j, ti=ti, bankQ=bankQ: e.transpose(
                        bankQ[:, j * 128:(j + 1) * 128],
                        qs[ti][:, c * 128:(c + 1) * 128], ident[:]))
                    if j == 3:
                        e_tr = ev("tensor", sT, fn, waits=w)
                    else:
                        silent("tensor", fn, waits=w)
                if rr == 1:
                    st2["qs_free"][ti] = e_tr
                e_qsT_r[rr] = ev("scalar", sS, lambda e, rr=rr, ti=ti,
                                 bankQ=bankQ: e.activation(
                    qsT[ti][:, rr * 4:(rr + 1) * 4, :].rearrange(
                        "p a b -> p (a b)"), bankQ[:, 0:512], AF.Identity),
                    waits=((sT, max(e_tr, st2["qsT_free"][ti])),))
                st2[fs] = e_qsT_r[rr]
            e_yev_s = 0
            e_yev_v = 0
            for rr in range(2):
                bankY = p6 if rr == 0 else p7
                fv = "f6v" if rr == 0 else "f7v"
                e_ymm = 0
                for j in range(4):
                    pr = rr * 4 + j
                    w = ()
                    if j == 0:
                        w = ((sV, st2[fv]), (sG, e_kvz[si]),
                             (sS, e_qsT_r[rr]))
                    fn = (lambda e, pr=pr, j=j, ti=ti, si=si, bankY=bankY:
                          e.matmul(
                        bankY[:, j * 128:(j + 1) * 128],
                        kvacc[si][:, pr * 129:pr * 129 + 128],
                        qsT[ti][:, pr, :], start=True, stop=True))
                    if j == 3:
                        e_ymm = ev("tensor", sT, fn, waits=w)
                    else:
                        silent("tensor", fn, waits=w)
                if rr == 1:
                    st2["qsT_free"][ti] = e_ymm
                e_yev_s = ev("scalar", sS, lambda e, rr=rr, ti=ti, bankY=bankY:
                             e.activation(
                    yTh[ti][:, rr * 4:(rr + 1) * 4, :].rearrange(
                        "p a b -> p (a b)"), bankY[:, 0:512], AF.Identity),
                    waits=((sT, max(e_ymm, st2["yT_free"][ti])), (sS, e_yev_s)))
                e_yev_v = ev("vector", sV, lambda e, rr=rr, ti=ti, bankY=bankY:
                             e.scalar_tensor_tensor(
                    yTl[ti][:, rr * 4:(rr + 1) * 4, :].rearrange(
                        "p a b -> p (a b)"),
                    bankY[:, 0:512], 1.0,
                    yTh[ti][:, rr * 4:(rr + 1) * 4, :].rearrange(
                        "p a b -> p (a b)"),
                    op0=OP.mult, op1=OP.subtract), waits=((sS, e_yev_s),))
                st2[fv] = e_yev_v
            e_yv[ti] = e_yev_v
        x1buf = x1[par]
        e_r1 = 0
        for ti in range(TG):
            pa, pb_ = qpb[ti]
            e_op = 0
            for c in range(NP):
                w = (((sV, max(st2["qb_evac"][ti], e_yv[ti],
                               st2["h_evac"][0], st2["h_evac"][1])),
                      (sS, max(st2["h_hi"][0], st2["h_hi"][1])))
                     if c == 0 else ())
                st = (c == 0)
                sp = (c == NP - 1)
                silent("tensor", lambda e, c=c, st=st, ti=ti, pa=pa: e.matmul(
                    pa[:, 0:512], yTh[ti][:, c, :], wo_h[:, c, 0:512],
                    start=st, stop=False), waits=w)
                silent("tensor", lambda e, c=c, st=st, ti=ti, pb_=pb_: e.matmul(
                    pb_[:, 0:512], yTh[ti][:, c, :], wo_h[:, c, 512:1024],
                    start=st, stop=False))
                silent("tensor", lambda e, c=c, ti=ti, pa=pa: e.matmul(
                    pa[:, 0:512], yTh[ti][:, c, :], wo_l[:, c, 0:512],
                    start=False, stop=False))
                silent("tensor", lambda e, c=c, ti=ti, pb_=pb_: e.matmul(
                    pb_[:, 0:512], yTh[ti][:, c, :], wo_l[:, c, 512:1024],
                    start=False, stop=False))
                silent("tensor", lambda e, c=c, sp=sp, ti=ti, pa=pa: e.matmul(
                    pa[:, 0:512], yTl[ti][:, c, :], wo_h[:, c, 0:512],
                    start=False, stop=sp))
                if c == NP - 1:
                    e_op = ev("tensor", sT, lambda e, c=c, ti=ti, pb_=pb_:
                              e.matmul(pb_[:, 0:512], yTl[ti][:, c, :],
                                       wo_h[:, c, 512:1024],
                                       start=False, stop=True))
                else:
                    silent("tensor", lambda e, c=c, ti=ti, pb_=pb_: e.matmul(
                        pb_[:, 0:512], yTl[ti][:, c, :], wo_h[:, c, 512:1024],
                        start=False, stop=False))
            st2["yT_free"][ti] = e_op
            tb = ti % 2
            silent("vector", lambda e, pa=pa, x1buf=x1buf, ti=ti, tb=tb:
                   e.scalar_tensor_tensor(
                x1buf[:, ti, 0:512], pa[:, 0:512], 1.0, xt[tb][:, 0:512],
                op0=OP.mult, op1=OP.add),
                waits=((sT, e_op), (sV, st2["xt_rdy"][tb])))
            e_r1 = ev("vector", sV, lambda e, pb_=pb_, x1buf=x1buf, ti=ti,
                      tb=tb: e.scalar_tensor_tensor(
                x1buf[:, ti, 512:1024], pb_[:, 0:512], 1.0,
                xt[tb][:, 512:1024], op0=OP.mult, op1=OP.add))
            st2["r1_evac"][ti] = e_r1
            st2["xt_free"][tb] = e_r1
            emit_ln1_tile(x1buf, ti, par, e_r1)

    def emit_mm2(c, cb, wb, e_w2c, e_hev, is_first, is_last):
        e_mm2 = 0
        w0 = ((sV, max(e_hev,
                       st2["ffnout_evac"] if is_first else 0,
                       max(st2["f4v"], st2["f5v"], st2["f6v"],
                           st2["f7v"]) if is_first else 0)),
              ((sS, max(st2["f4s"], st2["f5s"])) if is_first else (sS, 0)),
              e_w2c)
        emitted = False
        for ti in range(TG):
            for hv in range(2):
                hbuf = hh[cb] if hv == 0 else hl[cb]
                for nci in range(2):
                    stf = (c == 0 and hv == 0)
                    spf = (c == F4C - 1 and hv == 1)
                    w = w0 if not emitted else ()
                    emitted = True
                    fn = (lambda e, wb=wb, ti=ti, hbuf=hbuf, nci=nci,
                          stf=stf, spf=spf: e.matmul(
                        PB[4 + 2 * ti + nci][:, 0:512],
                        hbuf[:, ti * 128:(ti + 1) * 128],
                        w2s[wb][:, nci * 512:(nci + 1) * 512],
                        start=stf, stop=spf))
                    if ti == TG - 1 and hv == 1 and nci == 1:
                        e_mm2 = ev("tensor", sT, fn, waits=w)
                    else:
                        silent("tensor", fn, waits=w)
        st2["mm2_ev"][wb] = e_mm2
        return e_mm2

    def emit_ffn(si, gi, inject=None):
        par = gi % 2
        e_mm1 = 0
        pend_mm2 = None
        e_w2c = [0, 0, 0]
        for c in range(F4C):
            if inject is not None:
                inject(c)
            if c == 16:
                for ti in range(TG):
                    silent("vector", lambda e, ti=ti, par=par: e.tensor_tensor(
                        x1[par][:, ti, :], x1[par][:, ti, :], b2_r[:],
                        op=OP.add),
                        waits=((sV, st2["x1T_ready"][par]), e_bias2))
            cb = c % 2
            wb = c % 3
            if c < 3 and st2["w_pre"] is not None:
                e_w1, e_w2c[wb] = st2["w_pre"][c]
            else:
                e_w1 = dma_in(sW1[wb], lambda e, c=c, wb=wb: e.dma_start(
                    out=w1s[wb][:], in_=w1q[c, :, :, :]),
                    waits=(e_scr_all[0], e_scr_all[1],
                           (sT, st2["w1_free"][wb])))
                e_w2c[wb] = dma_in(sW2[wb], lambda e, c=c, wb=wb: e.dma_start(
                    out=w2s[wb][:], in_=w2q[c, :, :]),
                    waits=((sT, st2["mm2_ev"][wb]),))
            hp = PB[cb]
            for fc in range(NP):
                w = ()
                if fc == 0:
                    w = (e_w1,
                         (sV, max(st2["h_evac"][cb],
                                  st2["qb_evac"][0] if c < 2 else 0)),
                         (sS, max(st2["h_hi"][cb],
                                  st2["x1T_hi_rr"][par][0])))
                elif fc == 4:
                    w = ((sS, st2["x1T_hi_rr"][par][1]),)
                silent("tensor", lambda e, c=c, fc=fc, wb=wb, par=par, hp=hp:
                       e.matmul(hp[:, 0:TG * 128], w1s[wb][:, fc, :],
                                x1Th[par][:, fc, :],
                                start=(fc == 0), stop=False), waits=w)
            for fc in range(NP):
                w = ()
                if fc == 0:
                    w = ((sV, st2["x1T_rr"][par][0]),)
                elif fc == 4:
                    w = ((sV, st2["x1T_rr"][par][1]),)
                if fc == NP - 1:
                    e_mm1 = ev("tensor", sT, lambda e, c=c, fc=fc, wb=wb,
                               par=par, hp=hp: e.matmul(
                        hp[:, 0:TG * 128], w1s[wb][:, fc, :],
                        x1Tl[par][:, fc, :], start=False, stop=True), waits=w)
                else:
                    silent("tensor", lambda e, c=c, fc=fc, wb=wb, par=par,
                           hp=hp: e.matmul(
                        hp[:, 0:TG * 128], w1s[wb][:, fc, :],
                        x1Tl[par][:, fc, :], start=False, stop=False),
                        waits=w)
            st2["w1_free"][wb] = e_mm1
            e_hi = ev("scalar", sS, lambda e, c=c, cb=cb, hp=hp: e.activation(
                hh[cb][:], hp[:, 0:TG * 128], AF.Relu, bias=b1_c[:, c:c + 1]),
                waits=((sT, e_mm1), e_b1c))
            e_h32 = ev("scalar", sS, lambda e, c=c, cb=cb, hp=hp: e.activation(
                h32[cb][:], hp[:, 0:TG * 128], AF.Relu, bias=b1_c[:, c:c + 1]))
            st2["h_hi"][cb] = e_h32
            st2["h_evac"][cb] = ev("vector", sV, lambda e, cb=cb:
                                   e.scalar_tensor_tensor(
                hl[cb][:], h32[cb][:], 1.0, hh[cb][:],
                op0=OP.mult, op1=OP.subtract), waits=((sS, e_h32),))
            if pend_mm2 is not None:
                emit_mm2(*pend_mm2)
            pend_mm2 = (c, cb, wb, e_w2c[wb], st2["h_evac"][cb], c == 0,
                        c == F4C - 1)
        e_mm2 = emit_mm2(*pend_mm2)
        e_r2_t = [0, 0]
        for ti in range(TG):
            x1buf = x1[par]
            silent("vector", lambda e, ti=ti, x1buf=x1buf:
                   e.scalar_tensor_tensor(
                ot2[:, ti, 0:512], PB[4 + 2 * ti][:, 0:512], 1.0,
                x1buf[:, ti, 0:512], op0=OP.mult, op1=OP.add),
                waits=((sT, e_mm2), st2["ot_free"][ti % 2]))
            e_r2 = ev("vector", sV, lambda e, ti=ti, x1buf=x1buf:
                      e.scalar_tensor_tensor(
                ot2[:, ti, 512:1024], PB[5 + 2 * ti][:, 0:512], 1.0,
                x1buf[:, ti, 512:1024], op0=OP.mult, op1=OP.add))
            e_r2_t[ti] = e_r2
        st2["ffnout_evac"] = e_r2_t[1]
        if not st2["last_ffn"]:
            pre = []
            for c in range(3):
                wb2 = c % 3
                ew1 = dma_out(sWP1[wb2], lambda e, c=c, wb2=wb2: e.dma_start(
                    out=w1s[wb2][:], in_=w1q[c, :, :, :]),
                    waits=((sT, st2["w1_free"][wb2]),))
                ew2 = dma_out(sWP2[wb2], lambda e, c=c, wb2=wb2: e.dma_start(
                    out=w2s[wb2][:], in_=w2q[c, :, :]),
                    waits=((sT, st2["mm2_ev"][wb2]),))
                pre.append((ew1, ew2))
            st2["w_pre"] = pre
        st2["f4v"] = max(st2["f4v"], e_r2_t[0])
        st2["f5v"] = max(st2["f5v"], e_r2_t[0])
        st2["f6v"] = max(st2["f6v"], e_r2_t[1])
        st2["f7v"] = max(st2["f7v"], e_r2_t[1])
        st2["r2_ev"] = e_r2_t

    LN2_SQ = 1.0 / 512

    def emit_ln2_seg4_store(psi, pgi, e_s3):
        for ti in range(TG):
            ob = ti % 2
            e_o = ev("scalar", sS, lambda e, ti=ti: e.activation(
                ot2[:, ti, :], ot2[:, ti, :], AF.Identity,
                bias=lnt[:, 26 + ti:27 + ti], scale=lnt[:, 28 + ti:29 + ti]),
                waits=((sV, e_s3),))
            st2["lnt2_free"] = e_o
            st2["ot_free"][ob] = dma_out(sOT[ob], lambda e, psi=psi, pgi=pgi,
                                         ti=ti: e.dma_start(
                out=out[psi, pgi * TG + ti, :, :], in_=ot2[:, ti, :]),
                waits=((sS, e_o),))

    def mk_ln2_injector(psi, pgi, e_r2_pair):
        state = {}

        def inj(c):
            if c == 6:
                emit_ln_red(ot2, 0, 16, e_r2_pair[0])
                emit_ln_red(ot2, 1, 16, e_r2_pair[1])
                state["s1"] = emit_ln_seg1b(LN2_SQ, 16, st2["lnt2_free"])
            elif c == 7:
                e_d = 0
                for ti in range(TG):
                    e_d = emit_ln_seg2(ot2, ti, LN2_SQ, 16, qs[1][:],
                                       state["s1"],
                                       ((sT, st2["qs_free"][1]),))
                st2["qs_sfree"][1] = e_d
                state["s2"] = e_d
            elif c == 9:
                state["s3"] = emit_ln_seg3(LN2_SQ, 16, state["s2"])
            elif c == 14:
                emit_ln2_seg4_store(psi, pgi, state["s3"])
        return inj

    pend = None
    pprev = None
    pprev_r2 = (0, 0)
    groups = [(si, gi) for si in range(NSEQ) for gi in range(p2_groups)]
    for idx, (si, gi) in enumerate(groups):
        if gi == 0:
            st2["ksrep_dma"] = dma_in(sKS, lambda e, si=si: e.dma_start(
                out=ksrep[:],
                in_=ksum_hbm[si, :][None, :].broadcast_to([128, D])),
                waits=(e_ksum_out[si], (sV, st2["ksrep_free"])))
        if idx == 0:
            load_xt(si, gi, 0)
            load_xt(si, gi, 1)
        emit_qproj(si, gi)
        if idx == 0:
            emit_seq_boundary(1)
            st2["f4v"] = ev_kvevac
            st2["f5v"] = ev_kvevac
            st2["f6v"] = ev_kvevac
            st2["f7v"] = e_kr_last
        if pend is not None:
            emit_x1tr(pend[0], pend[1], tis=(1,))
            inj2 = (mk_ln2_injector(pprev[0], pprev[1], pprev_r2)
                    if pprev is not None else None)

            def inject(c, si=si, gi=gi, inj2=inj2):
                emit_qchain_tile(si, gi, c)
                if inj2 is not None:
                    inj2(c)
            emit_ffn(pend[0], pend[1], inject=inject)
            pprev = pend
            pprev_r2 = tuple(st2["r2_ev"])
        else:
            for cc in (2, 3, 4, 10, 11, 12):
                emit_qchain_tile(si, gi, cc)
        if idx + 1 < len(groups):
            nsi, ngi = groups[idx + 1]
            load_x_tile(nsi, ngi * TG + 0)
            load_x_tile(nsi, ngi * TG + 1)
        emit_attn_rest(si, gi)
        if idx + 1 < len(groups):
            nsi, ngi = groups[idx + 1]
            load_xt(nsi, ngi, 0)
            load_xt(nsi, ngi, 1)
        emit_x1tr(si, gi, tis=(0,))
        pend = (si, gi)
    if pend is not None:
        emit_x1tr(pend[0], pend[1], tis=(1,))
        inj2 = (mk_ln2_injector(pprev[0], pprev[1], pprev_r2)
                if pprev is not None else None)
        st2["last_ffn"] = True
        emit_ffn(pend[0], pend[1],
                 inject=(lambda c, inj2=inj2: inj2(c)) if inj2 else None)
        emit_ln_red(ot2, 0, 16, st2["r2_ev"][0])
        emit_ln_red(ot2, 1, 16, st2["r2_ev"][1])
        e_s1 = emit_ln_seg1b(LN2_SQ, 16, st2["lnt2_free"])
        e_d = 0
        for ti in range(TG):
            e_d = emit_ln_seg2(ot2, ti, LN2_SQ, 16, qs[1][:], e_s1,
                               ((sT, st2["qs_free"][1]),))
        st2["qs_sfree"][1] = e_d
        e_s3 = emit_ln_seg3(LN2_SQ, 16, e_d)
        emit_ln2_seg4_store(pend[0], pend[1], e_s3)
    silent("gpsimd", lambda e: e.nop(),
           waits=((sOT[0], sOT[0].n), (sOT[1], sOT[1].n),
                  (sQW[0], sQW[0].n), (sQW[1], sQW[1].n),
                  (sKSO, sKSO.n), (sGAM, sGAM.n)))

    with nc.Block() as block:
        @block.sync
        def _(e):
            sch.replay("sync", e)

        @block.tensor
        def _(e):
            sch.replay("tensor", e)

        @block.vector
        def _(e):
            sch.replay("vector", e)

        @block.scalar
        def _(e):
            sch.replay("scalar", e)

        @block.gpsimd
        def _(e):
            sch.replay("gpsimd", e)

    for g in reversed(alloc):
        g.__exit__(None, None, None)
    return nc


def stage_inputs(L0_attn_x, L0_attn_Wq, L0_attn_bq, L0_attn_Wk, L0_attn_bk,
                 L0_attn_Wv, L0_attn_bv, L0_attn_Wo, L0_attn_bo,
                 L0_ffn_W1, L0_ffn_b1, L0_ffn_W2, L0_ffn_b2):
    x = np.asarray(L0_attn_x, dtype=np.float32)
    shared = {
        "wqT": np.ascontiguousarray(np.asarray(L0_attn_Wq).T).reshape(NP, 128, D),
        "wkT": np.ascontiguousarray(np.asarray(L0_attn_Wk).T).reshape(NP, 128, D),
        "wvT": np.ascontiguousarray(np.asarray(L0_attn_Wv).T).reshape(NP, 128, D),
        "woT": np.ascontiguousarray(np.asarray(L0_attn_Wo).T).reshape(NP, 128, D),
        "w1r": np.ascontiguousarray(
            np.asarray(L0_ffn_W1).T.reshape(NP, 128, F4C, 128).transpose(
                2, 1, 0, 3)),
        "w2r": np.ascontiguousarray(np.asarray(L0_ffn_W2).T).reshape(
            F4C, 128, D),
        "bqd": np.ascontiguousarray(L0_attn_bq),
        "bkd": np.ascontiguousarray(L0_attn_bk),
        "bvd": np.ascontiguousarray(L0_attn_bv),
        "bod": np.ascontiguousarray(L0_attn_bo),
        "b1d": np.ascontiguousarray(np.asarray(L0_ffn_b1).reshape(F4C, 128).T),
        "b2d": np.ascontiguousarray(L0_ffn_b2),
    }
    in_maps = []
    for c in range(NCORES):
        xs = x[2 * c:2 * c + 2]
        m = dict(shared)
        m["xtm"] = np.ascontiguousarray(xs).reshape(NSEQ, NT, 128, D)
        m["xfm"] = np.ascontiguousarray(xs.transpose(0, 2, 1)).reshape(
            NSEQ, NP, 128, S)
        in_maps.append(m)
    return in_maps


def kernel(**inputs):
    in_maps = stage_inputs(**inputs)
    nc = build_program()
    res = run_bass_kernel_spmd(nc, in_maps, core_ids=list(range(NCORES)))
    parts = [np.asarray(r["out"]).reshape(NSEQ * S, D) for r in res.results]
    return np.concatenate(parts, axis=0).reshape(16, S, D)


# version 43 (final)
# speedup vs baseline: 1.0773x; 1.0773x over previous
import math
import numpy as np
import concourse.bass as bass
import concourse.mybir as mybir
from concourse.bass_utils import run_bass_kernel_spmd
from concourse.alu_op_type import AluOpType

F32 = mybir.dt.float32
BF16 = mybir.dt.bfloat16
AF = mybir.ActivationFunctionType
AX = mybir.AxisListType
OP = AluOpType

NCORES = 8
NSEQ = 2
S = 2048
D = 1024
H = 16
HD = 64
NP = 8
F4 = 4096
F4C = 32
NT = 16
TG = 2
NG = NT // TG
EPS_LN = 1e-5
EPS_Q = 1e-5
EPS_DEN = 1e-6
NW1 = F4 * D
NW2 = D * F4


class Sched:

    def __init__(self):
        self.ops = {k: [] for k in ("sync", "tensor", "vector", "scalar", "gpsimd")}

    def op(self, eng, fn, waits=(), inc=None, inc_n=1, self_wait=None):
        self.ops[eng].append((fn, tuple(waits), inc, inc_n, self_wait))

    def replay(self, name, e):
        seen = {}
        for fn, waits, inc, inc_n, self_wait in self.ops[name]:
            allw = waits if self_wait is None else (waits + (self_wait,))
            for sem, val in allw:
                if val <= 0 or seen.get(id(sem), -1) >= val:
                    continue
                e.wait_ge(sem.h, val)
                seen[id(sem)] = val
            ins = fn(e)
            if inc is not None:
                ins.then_inc(inc.h, inc_n)


class Sem:
    def __init__(self, h):
        self.h = h
        self.n = 0


def build_program(p2_groups=NG):
    nc = bass.Bass("TRN2", target_bir_lowering=False)

    xfm = nc.dram_tensor("xfm", [NSEQ, NP, 128, S], F32, kind="ExternalInput").ap()
    xtm = nc.dram_tensor("xtm", [NSEQ, NT, 128, D], F32, kind="ExternalInput").ap()
    wqT = nc.dram_tensor("wqT", [NP, 128, D], F32, kind="ExternalInput").ap()
    wkT = nc.dram_tensor("wkT", [NP, 128, D], F32, kind="ExternalInput").ap()
    wvT = nc.dram_tensor("wvT", [NP, 128, D], F32, kind="ExternalInput").ap()
    woT = nc.dram_tensor("woT", [NP, 128, D], F32, kind="ExternalInput").ap()
    w1r = nc.dram_tensor("w1r", [F4C, 128, NP, 128], F32, kind="ExternalInput").ap()
    w2r = nc.dram_tensor("w2r", [F4C, 128, D], F32, kind="ExternalInput").ap()
    bqd = nc.dram_tensor("bqd", [D], F32, kind="ExternalInput").ap()
    bkd = nc.dram_tensor("bkd", [D], F32, kind="ExternalInput").ap()
    bvd = nc.dram_tensor("bvd", [D], F32, kind="ExternalInput").ap()
    bod = nc.dram_tensor("bod", [D], F32, kind="ExternalInput").ap()
    b1d = nc.dram_tensor("b1d", [128, F4C], F32, kind="ExternalInput").ap()
    b2d = nc.dram_tensor("b2d", [D], F32, kind="ExternalInput").ap()
    out = nc.dram_tensor("out", [NSEQ, NT, 128, D], F32, kind="ExternalOutput").ap()
    w1q = nc.dram_tensor("w1q", [F4C, 128, NP, 128], BF16, kind="Internal").ap()
    w2q = nc.dram_tensor("w2q", [F4C, 128, D], BF16, kind="Internal").ap()
    ksum_hbm = nc.dram_tensor("ksum_hbm", [NSEQ, D], F32, kind="Internal").ap()
    gam_hbm = nc.dram_tensor("gam_hbm", [2, 1], F32, kind="Internal").ap()

    sch = Sched()
    alloc = []

    def sb(name, shape, dt):
        g = nc.sbuf_tensor(name, shape, dt)
        t = g.__enter__()
        alloc.append(g)
        return t

    wq_h = sb("wq_h", [128, NP, D], BF16)
    wq_l = sb("wq_l", [128, NP, D], BF16)
    wo_h = sb("wo_h", [128, NP, D], BF16)
    wo_l = sb("wo_l", [128, NP, D], BF16)
    xg = [sb("xg0", [128, NP, 128], F32), sb("xg1", [128, NP, 128], F32)]
    xgh = [sb(f"xgh{i}", [128, NP, 128], BF16) for i in range(4)]
    xgl = [sb(f"xgl{i}", [128, NP, 128], BF16) for i in range(4)]
    kvacc = [sb("kv0", [128, 8 * 129], F32), sb("kv1", [128, 8 * 129], F32)]
    ident = sb("ident", [128, 128], F32)
    ksrow = sb("ksrow", [8, 128], F32)
    b1_c = sb("b1_c", [128, F4C], F32)
    gsum2 = sb("gsum2", [2, 1], F32)

    p1_guards = []

    def sb1(name, shape, dt):
        g = nc.sbuf_tensor(name, shape, dt)
        t = g.__enter__()
        p1_guards.append(g)
        return t

    wk_h = sb1("wk_h", [128, NP, D], BF16)
    wk_l = sb1("wk_l", [128, NP, D], BF16)
    wv_h = sb1("wv_h", [128, NP, D], BF16)
    wv_l = sb1("wv_l", [128, NP, D], BF16)
    wrc = [sb1("wrc0", [128, D], F32), sb1("wrc1", [128, D], F32)]
    bk_r = sb1("bk_r", [128, D], F32)
    bv_r = sb1("bv_r", [128, D], F32)
    kkern = [sb1("kkern0", [128, D], F32), sb1("kkern1", [128, D], F32)]
    vprime = [sb1("vprime0", [128, 8 * 129], F32),
              sb1("vprime1", [128, 8 * 129], F32)]
    scr0 = sb1("scr0", [128, D], BF16)
    nm1 = sb1("nm1", [128, H], F32)
    wsl = [sb1("wsl0", [128, NP * 128], F32), sb1("wsl1", [128, NP * 128], F32)]
    wslq = [sb1("wslq0", [128, NP * 128], BF16),
            sb1("wslq1", [128, NP * 128], BF16)]
    absps = sb1("absps", [128, 72], F32)

    def ps(name):
        g = nc.psum_tensor(name, [128, 512], F32)
        t = g.__enter__()
        alloc.append(g)
        return t

    PB = [ps(f"p{i}") for i in range(8)]
    p0, p1, p2, p3, p4, p5, p6, p7 = PB

    def mksem(name):
        g = nc.semaphore(name)
        h = g.__enter__()
        alloc.append(g)
        return Sem(h)

    sT = mksem("sT")
    sV = mksem("sV")
    sS = mksem("sS")
    sG = mksem("sG")
    sB = mksem("sB")
    sXG = [mksem("sXG0"), mksem("sXG1")]
    sXT = [mksem("sXT0"), mksem("sXT1")]
    sWRC = [mksem("sWRC0"), mksem("sWRC1")]
    sWSL = [mksem("sWSL0"), mksem("sWSL1")]
    sWG = [mksem("sWG0"), mksem("sWG1")]
    sWP1 = [mksem(f"sWP1{i}") for i in range(3)]
    sWP2 = [mksem(f"sWP2{i}") for i in range(3)]
    sW1 = [mksem("sW10"), mksem("sW11"), mksem("sW12")]
    sW2 = [mksem("sW20"), mksem("sW21"), mksem("sW22")]
    sKS = mksem("sKS")
    sOT = [mksem("sOT0"), mksem("sOT1")]
    sQW = [mksem("sQW0"), mksem("sQW1")]
    sKSO = mksem("sKSO")
    sGAM = mksem("sGAM")

    def dma_in(sem, fn, waits=()):
        sch.op("sync", fn, waits=waits, inc=sem, inc_n=16)
        sem.n += 16
        return (sem, sem.n)

    def dma_out(sem, fn, waits=()):
        sch.op("gpsimd", fn, waits=waits, inc=sem, inc_n=16,
               self_wait=(sG, sG.n))
        sem.n += 16
        return (sem, sem.n)

    ENG_SEM = {"vector": sV, "scalar": sS, "gpsimd": sG}

    def ev(eng, sem, fn, waits=()):
        es = ENG_SEM.get(eng)
        if es is not None:
            assert sem is es, f"{eng} ops must inc their engine sem"
            sw = (es, es.n)
            sch.op(eng, fn, waits=waits, inc=sem, inc_n=1, self_wait=sw)
        else:
            sch.op(eng, fn, waits=waits, inc=sem, inc_n=1)
        sem.n += 1
        return sem.n

    def silent(eng, fn, waits=()):
        es = ENG_SEM.get(eng)
        if es is not None:
            sw = (es, es.n)
            sch.op(eng, fn, waits=waits, inc=es, inc_n=1, self_wait=sw)
            es.n += 1
        else:
            sch.op(eng, fn, waits=waits)

    eB = (sB, 0)
    for bsrc, bdst in ((bkd, bk_r), (bvd, bv_r)):
        eB = dma_in(sB, lambda e, s=bsrc, d=bdst: e.dma_start(
            out=d[:], in_=s[None, :].broadcast_to([128, D])), waits=(eB,))
    e_bias1 = eB
    eB = dma_in(sB, lambda e: e.dma_start(out=b1_c[:], in_=b1d[:, :]),
                waits=(eB,))
    e_b1c = eB

    silent("gpsimd", lambda e: e.memset(ident[:], 0.0))
    silent("gpsimd", lambda e: e.affine_select(
        out=ident[:], in_=ident[:], compare_op=OP.not_equal, fill=1.0,
        base=0, pattern=[[-1, 128]], channel_multiplier=1))
    silent("gpsimd", lambda e: e.memset(
        vprime[0][:].rearrange("p (a b) -> p a b", b=129)[:, :, 128:129], 1.0))
    e_gconst = ev("gpsimd", sG, lambda e: e.memset(
        vprime[1][:].rearrange("p (a b) -> p a b", b=129)[:, :, 128:129], 1.0))

    e_wrc_v = [0, 0]
    qslab_v = [0, 0]
    e_wrc_s = [0, 0]
    wg_state = {"k": 0}

    def split_chunk_sync(src, dsth, dstl, c):
        cb = c % 2
        e_d = dma_in(sWRC[cb], lambda e, src=src, c=c, cb=cb: e.dma_start(
            out=wrc[cb][:], in_=src[c, :, :]),
            waits=((sV, e_wrc_v[cb]), (sS, e_wrc_s[cb])))
        e_wrc_s[cb] = ev("scalar", sS, lambda e, c=c, cb=cb, dsth=dsth:
                         e.activation(
            dsth[:, c, :], wrc[cb][:], AF.Identity), waits=(e_d,))
        e_l = ev("vector", sV, lambda e, c=c, cb=cb, dsth=dsth, dstl=dstl:
                 e.scalar_tensor_tensor(
            dstl[:, c, :], wrc[cb][:], 1.0, dsth[:, c, :],
            op0=OP.mult, op1=OP.subtract), waits=((sS, e_wrc_s[cb]),))
        e_wrc_v[cb] = e_l
        return e_l

    def split_chunk_g(src, dsth, dstl, c):
        b = wg_state["k"] % 2
        wg_state["k"] += 1
        e_d = dma_out(sWG[b], lambda e, src=src, c=c, b=b: e.dma_start(
            out=wsl[b][:], in_=src[c, :, :]), waits=((sV, qslab_v[b]),))
        e_h = ev("scalar", sS, lambda e, c=c, b=b, dsth=dsth: e.activation(
            dsth[:, c, :], wsl[b][:], AF.Identity), waits=(e_d,))
        e_l = ev("vector", sV, lambda e, c=c, b=b, dsth=dsth, dstl=dstl:
                 e.scalar_tensor_tensor(
            dstl[:, c, :], wsl[b][:], 1.0, dsth[:, c, :],
            op0=OP.mult, op1=OP.subtract), waits=((sS, e_h),))
        qslab_v[b] = e_l
        return e_l

    xg_state = {"dma": [(sXG[0], 0), (sXG[1], 0)], "split": [0, 0, 0, 0],
                "splitb2": [0, 0], "tfree": [0, 0, 0, 0]}

    def load_x_tile(si, i):
        bp = i % 2
        p4 = i % 4
        t0 = i * 128
        e_d = dma_in(sXG[bp], lambda e, si=si, t0=t0, bp=bp: e.dma_start(
            out=xg[bp][:],
            in_=xfm[si, :, :, t0:t0 + 128].rearrange("c p t -> p c t")),
            waits=((sV, xg_state["splitb2"][bp]),))
        e_h = ev("scalar", sS, lambda e, bp=bp, p4=p4: e.activation(
            xgh[p4][:].rearrange("p c t -> p (c t)"),
            xg[bp][:].rearrange("p c t -> p (c t)"), AF.Identity),
            waits=(e_d, (sT, xg_state["tfree"][p4])))
        e_s = ev("vector", sV, lambda e, bp=bp, p4=p4: e.scalar_tensor_tensor(
            xgl[p4][:].rearrange("p c t -> p (c t)"),
            xg[bp][:].rearrange("p c t -> p (c t)"), 1.0,
            xgh[p4][:].rearrange("p c t -> p (c t)"),
            op0=OP.mult, op1=OP.subtract), waits=((sS, e_h),))
        xg_state["dma"][bp] = e_d
        xg_state["split"][p4] = e_s
        xg_state["splitb2"][bp] = e_s
        return e_s

    def proj3(bp, wh, wl, pa, pb_, first_waits, set_tfree=False,
              per_chunk=None):
        last = 0
        for c in range(NP):
            w = first_waits if c == 0 else ()
            if per_chunk is not None:
                w = w + ((sV, per_chunk[c]),)
            st = (c == 0)
            sp = (c == NP - 1)
            silent("tensor", lambda e, c=c, st=st, bp=bp, wh=wh, pa=pa: e.matmul(
                pa[:, 0:512], xgh[bp][:, c, :], wh[:, c, 0:512],
                start=st, stop=False), waits=w)
            silent("tensor", lambda e, c=c, st=st, bp=bp, wh=wh, pb_=pb_: e.matmul(
                pb_[:, 0:512], xgh[bp][:, c, :], wh[:, c, 512:1024],
                start=st, stop=False))
            silent("tensor", lambda e, c=c, bp=bp, wl=wl, pa=pa: e.matmul(
                pa[:, 0:512], xgh[bp][:, c, :], wl[:, c, 0:512],
                start=False, stop=False))
            silent("tensor", lambda e, c=c, bp=bp, wl=wl, pb_=pb_: e.matmul(
                pb_[:, 0:512], xgh[bp][:, c, :], wl[:, c, 512:1024],
                start=False, stop=False))
            silent("tensor", lambda e, c=c, sp=sp, bp=bp, wh=wh, pa=pa: e.matmul(
                pa[:, 0:512], xgl[bp][:, c, :], wh[:, c, 0:512],
                start=False, stop=sp))
            if c == NP - 1:
                last = ev("tensor", sT, lambda e, c=c, bp=bp, wh=wh, pb_=pb_:
                          e.matmul(pb_[:, 0:512], xgl[bp][:, c, :],
                                   wh[:, c, 512:1024], start=False, stop=True))
            else:
                silent("tensor", lambda e, c=c, bp=bp, wh=wh, pb_=pb_: e.matmul(
                    pb_[:, 0:512], xgl[bp][:, c, :], wh[:, c, 512:1024],
                    start=False, stop=False))
        if set_tfree:
            xg_state["tfree"][bp] = last
        return last

    qst = {"gi": 0, "qi": 0, "e_thr": 0, "e_gam_p7": 0, "g_used": 0,
           "slab_s": [0, 0],
           "slabq_out": [(sQW[0], 0), (sQW[1], 0)]}

    def q_src(i):
        if i < 32:
            return w1r[i, :, :, :].rearrange("p c d -> p (c d)")
        return w2r[i - 32, :, :]

    def emit_gamma_slab():
        i = qst["gi"]
        if i >= 64:
            return
        qst["gi"] += 1
        b = i % 2
        e_d = dma_in(sWSL[b], lambda e, i=i, b=b: e.dma_start(
            out=wsl[b][:], in_=q_src(i)), waits=((sV, qslab_v[b]),))
        qslab_v[b] = ev("vector", sV, lambda e, i=i, b=b: e.tensor_reduce(
            absps[:, i:i + 1], wsl[b][:], axis=AX.X, op=OP.add,
            apply_absolute_value=True), waits=(e_d,))

    def emit_gamma_finalize():
        nonlocal eB
        ev("vector", sV, lambda e: e.tensor_reduce(
            absps[:, 64:65], absps[:, 0:32], axis=AX.X, op=OP.add),
            waits=((sV, max(qslab_v)),))
        e2 = ev("vector", sV, lambda e: e.tensor_reduce(
            absps[:, 65:66], absps[:, 32:64], axis=AX.X, op=OP.add))
        e_t = ev("tensor", sT, lambda e: e.transpose(
            p7[0:2, 0:128], absps[:, 64:66], ident[:]),
            waits=((sV, e2), (sG, e_gconst)))
        e3 = ev("vector", sV, lambda e: e.tensor_reduce(
            gsum2[:, 0:1], p7[0:2, 0:128], axis=AX.X, op=OP.add),
            waits=((sT, e_t),))
        qst["e_gam_p7"] = e3
        e_g = dma_out(sGAM, lambda e: e.dma_start(
            out=gam_hbm[:, :], in_=gsum2[:, :]), waits=((sV, e3),))
        eB = dma_in(sB, lambda e: e.dma_start(
            out=absps[:, 66:67],
            in_=gam_hbm[0, :][None, :].broadcast_to([128, 1])),
            waits=(eB, e_g))
        eB = dma_in(sB, lambda e: e.dma_start(
            out=absps[:, 67:68],
            in_=gam_hbm[1, :][None, :].broadcast_to([128, 1])), waits=(eB,))
        silent("vector", lambda e: e.tensor_scalar(
            absps[:, 68:69], absps[:, 66:67], 0.5 / NW1, 0.5 * EPS_Q,
            op0=OP.mult, op1=OP.add), waits=(eB,))
        silent("vector", lambda e: e.tensor_scalar(
            absps[:, 69:70], absps[:, 68:69], -1.0, None, op0=OP.mult))
        silent("vector", lambda e: e.tensor_scalar(
            absps[:, 70:71], absps[:, 67:68], 0.5 / NW2, 0.5 * EPS_Q,
            op0=OP.mult, op1=OP.add))
        qst["e_thr"] = ev("vector", sV, lambda e: e.tensor_scalar(
            absps[:, 71:72], absps[:, 70:71], -1.0, None, op0=OP.mult))

    def emit_quant_slab():
        i = qst["qi"]
        if i >= 64:
            return
        qst["qi"] += 1
        b = i % 2
        if i < 32:
            dst = w1q[i, :, :, :].rearrange("p c d -> p (c d)")
            tpos, tneg = 68, 69
        else:
            dst = w2q[i - 32, :, :]
            tpos, tneg = 70, 71
        e_d = dma_in(sWSL[b], lambda e, i=i, b=b: e.dma_start(
            out=wsl[b][:], in_=q_src(i)), waits=((sV, qslab_v[b]),))
        silent("vector", lambda e, b=b, tneg=tneg: e.tensor_scalar(
            scr0[:], wsl[b][:], absps[:, tneg:tneg + 1], None, op0=OP.is_lt),
            waits=(e_d, (sV, qst["e_thr"]), qst["slabq_out"][b]))
        qslab_v[b] = ev("vector", sV, lambda e, b=b, tpos=tpos:
                        e.scalar_tensor_tensor(
            wslq[b][:], wsl[b][:], absps[:, tpos:tpos + 1], scr0[:],
            op0=OP.is_gt, op1=OP.subtract))
        qst["slabq_out"][b] = dma_out(
            sQW[b], lambda e, b=b, d=dst: e.dma_start(out=d, in_=wslq[b][:]),
            waits=((sV, qslab_v[b]),))

    ev_kevac = 0
    ev_vevac = 0
    ev_kk = 0
    ev_kvevac = 0
    e_kvmm = 0
    e_kvmm_kb = [0, 0]
    e_ksum_out = [(sKSO, 0), (sKSO, 0)]
    e_kvz = [0, 0]
    e_kr_last = 0

    load_x_tile(0, 0)
    wk_evs = []
    wv_evs = []
    for c in range(NP):
        wk_evs.append(split_chunk_sync(wkT, wk_h, wk_l, c))
        wv_evs.append(split_chunk_g(wvT, wv_h, wv_l, c))
    e_wkv = wk_evs[-1]
    wqo_pend = [(wqT, wq_h, wq_l, c) for c in range(NP)] + \
               [(woT, wo_h, wo_l, c) for c in range(NP)]
    pend_kv = None

    def emit_kv(i, kb, e_kk, e_vev):
        nonlocal e_kvmm
        for pr in range(8):
            bank = PB[4 + pr // 3]
            off = (pr % 3) * 129
            w = (((sS, e_kk), (sV, max(e_vev, ev_kvevac)))
                 if pr == 0 else ())
            fn = (lambda e, pr=pr, bank=bank, off=off, i=i, kb=kb: e.matmul(
                bank[:, off:off + 129],
                kkern[kb][:, pr * 128:(pr + 1) * 128],
                vprime[kb][:, pr * 129:(pr + 1) * 129],
                start=(i == 0 and pr % 3 == 0),
                stop=(i == NT - 1 and (pr % 3 == 2 or pr == 7))))
            if pr == 7:
                e_kvmm = ev("tensor", sT, fn, waits=w)
                e_kvmm_kb[kb] = e_kvmm
            else:
                silent("tensor", fn, waits=w)

    def emit_seq_boundary(si):
        nonlocal ev_kvevac, e_kr_last
        kva = kvacc[si]
        silent("vector", lambda e, kva=kva: e.tensor_copy(
            kva[:, 0:387], p4[:, 0:387]), waits=((sT, e_kvmm),))
        silent("vector", lambda e, kva=kva: e.tensor_copy(
            kva[:, 387:774], p5[:, 0:387]))
        ev_kvevac = ev("vector", sV, lambda e, kva=kva: e.tensor_copy(
            kva[:, 774:1032], p6[:, 0:258]))
        for pr in range(8):
            col = pr * 129
            silent("gpsimd", lambda e, kva=kva, col=col: e.memset(
                kva[0:64, col + 64:col + 128], 0.0), waits=((sV, ev_kvevac),))
            if pr == 7:
                e_kvz[si] = ev("gpsimd", sG, lambda e, kva=kva, col=col:
                               e.memset(kva[64:128, col:col + 64], 0.0))
            else:
                silent("gpsimd", lambda e, kva=kva, col=col: e.memset(
                    kva[64:128, col:col + 64], 0.0))
        e_t = ev("tensor", sT, lambda e, kva=kva: e.transpose(
            p7[0:8, 0:128],
            kva[:].rearrange("p (a b) -> p a b", b=129)[:, :, 128],
            ident[:]), waits=((sV, max(ev_kvevac, qst["e_gam_p7"])),))
        e_kr_last = ev("vector", sV, lambda e: e.tensor_copy(
            ksrow[:, :], p7[0:8, 0:128]),
            waits=((sT, e_t), (sKSO, 16 * si)))
        e_ksum_out[si] = dma_out(sKSO, lambda e, si=si: e.dma_start(
            out=ksum_hbm[si, :].rearrange("(a b) -> a b", a=8),
            in_=ksrow[:, :]),
            waits=((sV, e_kr_last), (sKSO, 16 * si)))

    for si in range(NSEQ):
        for i in range(NT):
            bp = i % 4
            if i < NT - 1:
                load_x_tile(si, i + 1)
            elif si == 0:
                load_x_tile(1, 0)
            if si == 0:
                for _ in range(2):
                    if wqo_pend:
                        split_chunk_g(*wqo_pend.pop(0))
                for _ in range(4):
                    emit_gamma_slab()
            elif not (si == 1 and i == 0):
                for _ in range(5 if i in (1, 2, 3) else 4):
                    emit_quant_slab()
            first_tile = (si == 0 and i == 0)
            e_kp = proj3(bp, wk_h, wk_l, p0, p1,
                         (e_bias1,
                          (sV, max(xg_state["split"][bp], ev_kevac, e_wkv))),
                         set_tfree=False,
                         per_chunk=wk_evs if first_tile else None)
            e_vpmm = proj3(bp, wv_h, wv_l, p2, p3, ((sV, ev_vevac),),
                           set_tfree=True,
                           per_chunk=wv_evs if first_tile else None)
            if pend_kv is not None:
                emit_kv(*pend_kv)
                pend_kv = None
            kb = i % 2
            silent("vector", lambda e, kb=kb: e.scalar_tensor_tensor(
                kkern[kb][:, 0:512], p0[:, 0:512], 1.0, bk_r[:, 0:512],
                op0=OP.mult, op1=OP.add),
                waits=((sT, max(e_kp, e_kvmm_kb[kb])),))
            ev_kevac = ev("vector", sV, lambda e, kb=kb: e.scalar_tensor_tensor(
                kkern[kb][:, 512:1024], p1[:, 0:512], 1.0, bk_r[:, 512:1024],
                op0=OP.mult, op1=OP.add))
            silent("vector", lambda e, kb=kb: e.tensor_reduce(
                nm1[:].rearrange("p (h o) -> p h o", o=1),
                kkern[kb][:].rearrange("p (h d) -> p h d", d=HD),
                axis=AX.X, op=OP.max, negate=True))
            e_sub = ev("vector", sV, lambda e, kb=kb: e.tensor_tensor(
                kkern[kb][:].rearrange("p (h d) -> p h d", d=HD),
                kkern[kb][:].rearrange("p (h d) -> p h d", d=HD),
                nm1[:].rearrange("p (h o) -> p h o", o=1).broadcast_to(
                    [128, H, HD]), op=OP.add))
            ev_kk = ev("scalar", sS, lambda e, kb=kb: e.activation(
                kkern[kb][:], kkern[kb][:], AF.Exp), waits=((sV, e_sub),))
            silent("vector", lambda e, kb=kb: e.scalar_tensor_tensor(
                vprime[kb][:].rearrange("p (a b) -> p a b", b=129)
                [:, 0:4, 0:128],
                p2[:, 0:512], 1.0, bv_r[:, 0:512], op0=OP.mult, op1=OP.add),
                waits=((sT, max(e_vpmm, e_kvmm_kb[kb])), (sG, e_gconst)))
            ev_vevac = ev("vector", sV, lambda e, kb=kb: e.scalar_tensor_tensor(
                vprime[kb][:].rearrange("p (a b) -> p a b", b=129)
                [:, 4:8, 0:128],
                p3[:, 0:512], 1.0, bv_r[:, 512:1024], op0=OP.mult, op1=OP.add))
            pend_kv = (i, kb, ev_kk, ev_vevac)
            if si == 0 and i == NT - 1:
                emit_gamma_finalize()
            if si == 1 and i == 0:
                emit_seq_boundary(0)
                emit_quant_slab()
        if pend_kv is not None:
            emit_kv(*pend_kv)
            pend_kv = None

    while qst["qi"] < 64:
        emit_quant_slab()
    e_scr_all = (qst["slabq_out"][0], qst["slabq_out"][1])
    eT1 = sT.n
    eV1 = sV.n
    eS1 = sS.n
    eG1 = sG.n

    for g in reversed(p1_guards):
        g.__exit__(None, None, None)

    bq_r = sb("bq_r", [128, D], F32)
    bo_r = sb("bo_r", [128, D], F32)
    b2_r = sb("b2_r", [128, D], F32)
    ksrep = sb("ksrep", [128, D], F32)
    xt = [sb("xt0", [128, D], F32), sb("xt1", [128, D], F32)]
    qb = [sb("qb0", [128, D], F32), sb("qb1", [128, D], F32)]
    qs = [sb("qs0", [128, D], F32), sb("qs1", [128, D], F32)]
    qsT = [qb[0].rearrange("p (a b) -> p a b", b=128),
           qb[1].rearrange("p (a b) -> p a b", b=128)]
    dt = qb
    nm = [sb("nm0", [128, H], F32), sb("nm1b", [128, H], F32)]
    den = [sb("den0", [128, H], F32), sb("den1", [128, H], F32)]
    yTh = [sb("yTh0", [128, NP, 128], BF16), sb("yTh1", [128, NP, 128], BF16)]
    yTl = [sb("yTl0", [128, NP, 128], BF16), sb("yTl1", [128, NP, 128], BF16)]
    x1 = [sb("x10", [128, TG, D], F32), sb("x11", [128, TG, D], F32)]
    x1Th = [sb("x1Th0", [128, NP, TG * 128], BF16),
            sb("x1Th1", [128, NP, TG * 128], BF16)]
    x1Tl = [sb("x1Tl0", [128, NP, TG * 128], BF16),
            sb("x1Tl1", [128, NP, TG * 128], BF16)]
    w1s = [sb(f"w1s{i}", [128, NP, 128], BF16) for i in range(3)]
    w2s = [sb(f"w2s{i}", [128, D], BF16) for i in range(3)]
    hh = [sb("hh0", [128, TG * 128], BF16), sb("hh1", [128, TG * 128], BF16)]
    hl = [sb("hl0", [128, TG * 128], BF16), sb("hl1", [128, TG * 128], BF16)]
    h32 = [sb("h320", [128, TG * 128], F32), sb("h321", [128, TG * 128], F32)]
    lnt = sb("lnt", [128, 36], F32)
    ot2 = sb("ot2", [128, TG, D], F32)

    e_eps = ev("gpsimd", sG, lambda e: e.memset(lnt[:, 32:33], EPS_LN),
               waits=((sT, eT1), (sV, eV1), (sS, eS1)))

    st2 = {
        "r1_evac": [ev_kevac, ev_vevac],
        "qb_evac": [0, 0],
        "qb_free": [eS1, eS1],
        "ffnout_evac": 0,
        "h_evac": [0, 0],
        "h_hi": [0, 0],
        "w1_free": [eT1, eT1, eT1],
        "e_mm1_last": eT1,
        "x1T_ready": [0, 0],
        "x1T_rr": [[0, 0], [0, 0]],
        "x1T_hi_rr": [[0, 0], [0, 0]],
        "x1_ready": [0, 0, 0, 0],
        "x1T_free": eT1,
        "xt_rdy": [0, 0],
        "xt_free": [0, 0],
        "ot_free": [(sOT[0], 0), (sOT[1], 0)],
        "ksrep_dma": (sKS, 0),
        "ksrep_free": 0,
        "f4v": ev_kvevac, "f5v": ev_kvevac,
        "f4s": eS1, "f5s": eS1,
        "f6v": ev_kvevac, "f7v": e_kr_last,
        "qs_free": [eT1, eT1],
        "qs_sfree": [eS1, eS1],
        "qsc": [0, 0],
        "qc_exp": [0, 0],
        "w_pre": None,
        "last_ffn": False,
        "qsT_free": [eT1, eT1],
        "yT_free": [eT1, eT1],
        "mm2_ev": [eT1, eT1, eT1],
        "lnt1_free": [eS1, eS1],
        "lnt2_free": eS1,
        "r2_ev": [0, 0],
    }

    load_x_tile(0, 0)
    load_x_tile(0, 1)

    eB = dma_in(sB, lambda e: e.dma_start(
        out=bq_r[:], in_=bqd[None, :].broadcast_to([128, D])),
        waits=(eB, (sT, eT1), (sV, eV1), (sS, eS1), (sG, eG1),
               e_scr_all[0], e_scr_all[1], e_ksum_out[1], (sGAM, 16)))
    e_bias_q = eB
    eB = dma_in(sB, lambda e: e.dma_start(
        out=bo_r[:], in_=bod[None, :].broadcast_to([128, D])), waits=(eB,))
    e_bias_o = eB
    eB = dma_in(sB, lambda e: e.dma_start(
        out=b2_r[:], in_=b2d[None, :].broadcast_to([128, D])), waits=(eB,))
    e_bias2 = eB

    def emit_ln_red(buf3, ti, cb, e_prior):
        return ev("vector", sV, lambda e, buf3=buf3, ti=ti, cb=cb:
                  e.tensor_reduce(
            lnt[:, cb + ti:cb + ti + 1], buf3[:, ti, :],
            axis=AX.X, op=OP.add), waits=((sV, e_prior),))

    def emit_ln_seg1b(sq_scale, cb, e_lnt_free):
        silent("vector", lambda e, cb=cb: e.tensor_scalar(
            lnt[:, cb + 2:cb + 4], lnt[:, cb:cb + 2], 1.0 / D, None,
            op0=OP.mult))
        return ev("vector", sV, lambda e, cb=cb, sq_scale=sq_scale:
                  e.tensor_scalar(
            lnt[:, cb + 14:cb + 16], lnt[:, cb + 2:cb + 4], -sq_scale, None,
            op0=OP.mult), waits=((sS, e_lnt_free),))

    def emit_ln_seg2(buf3, ti, sq_scale, cb, dump, e_seg1, extra_waits):
        return ev("scalar", sS, lambda e, buf3=buf3, ti=ti, cb=cb,
                  sq_scale=sq_scale, dump=dump: e.activation(
            dump, buf3[:, ti, :], AF.Square, scale=sq_scale,
            bias=lnt[:, cb + 14 + ti:cb + 15 + ti],
            accum_out=lnt[:, cb + 4 + ti:cb + 5 + ti]),
            waits=((sV, e_seg1),) + extra_waits)

    def emit_ln_seg3(sq_scale, cb, e_seg2):
        e_v1 = ev("vector", sV, lambda e, cb=cb, sq_scale=sq_scale:
                  e.tensor_scalar(
            lnt[:, cb + 4:cb + 6], lnt[:, cb + 4:cb + 6],
            1.0 / (D * sq_scale * sq_scale), EPS_LN,
            op0=OP.mult, op1=OP.add), waits=((sS, e_seg2),))
        silent("scalar", lambda e, cb=cb: e.activation(
            lnt[:, cb + 6:cb + 8], lnt[:, cb + 4:cb + 6], AF.Ln),
            waits=((sV, e_v1),))
        e_ex = ev("scalar", sS, lambda e, cb=cb: e.activation(
            lnt[:, cb + 12:cb + 14], lnt[:, cb + 6:cb + 8], AF.Exp,
            scale=-0.5))
        return ev("vector", sV, lambda e, cb=cb: e.scalar_tensor_tensor(
            lnt[:, cb + 10:cb + 12], lnt[:, cb + 2:cb + 4], -1.0,
            lnt[:, cb + 12:cb + 14], op0=OP.mult, op1=OP.mult),
            waits=((sS, e_ex),))

    def load_xt(si, gi, ti):
        tb = ti % 2
        e_d = dma_in(sXT[tb], lambda e, si=si, gi=gi, ti=ti,
                     tb=tb: e.dma_start(
            out=xt[tb][:], in_=xtm[si, gi * TG + ti, :, :]),
            waits=((sV, st2["xt_free"][tb]),))
        st2["xt_rdy"][tb] = ev("vector", sV, lambda e, tb=tb: e.tensor_tensor(
            xt[tb][:], xt[tb][:], bo_r[:], op=OP.add),
            waits=(e_d, e_bias_o))

    qpb = [(p0, p1), (p2, p3)]

    def emit_qproj(si, gi):
        for ti in range(TG):
            bp = (gi * TG + ti) % 4
            pa, pb_ = qpb[ti]
            e_qp = proj3(bp, wq_h, wq_l, pa, pb_,
                         ((sV, max(xg_state["split"][bp],
                                   st2["r1_evac"][ti])),),
                         set_tfree=True)
            silent("vector", lambda e, ti=ti, pa=pa: e.scalar_tensor_tensor(
                qb[ti][:, 0:512], pa[:, 0:512], 1.0, bq_r[:, 0:512],
                op0=OP.mult, op1=OP.add), waits=((sT, e_qp), e_bias_q))
            st2["qb_evac"][ti] = ev("vector", sV, lambda e, ti=ti, pb_=pb_:
                                    e.scalar_tensor_tensor(
                qb[ti][:, 512:1024], pb_[:, 0:512], 1.0, bq_r[:, 512:1024],
                op0=OP.mult, op1=OP.add))
    def emit_qchain_tile(si, gi, c):
        if c in (2, 3, 4):
            ti, step = 0, c - 2
        elif c in (10, 11, 12):
            ti, step = 1, c - 10
        else:
            return
        if step == 0:
            ev("vector", sV, lambda e, ti=ti: e.tensor_reduce(
                nm[ti][:].rearrange("p (h o) -> p h o", o=1),
                qb[ti][:].rearrange("p (h d) -> p h d", d=HD),
                axis=AX.X, op=OP.max, negate=True),
                waits=((sV, st2["qb_evac"][ti]),))
            e_sub = ev("vector", sV, lambda e, ti=ti: e.tensor_tensor(
                qs[ti][:].rearrange("p (h d) -> p h d", d=HD),
                qb[ti][:].rearrange("p (h d) -> p h d", d=HD),
                nm[ti][:].rearrange("p (h o) -> p h o", o=1).broadcast_to(
                    [128, H, HD]), op=OP.add),
                waits=((sT, st2["qs_free"][ti]), (sS, st2["qs_sfree"][ti])))
            e_exp = ev("scalar", sS, lambda e, ti=ti: e.activation(
                qs[ti][:], qs[ti][:], AF.Exp), waits=((sV, e_sub),))
            st2["qs_sfree"][ti] = e_exp
            st2["qc_exp"][ti] = e_exp
        elif step == 1:
            silent("vector", lambda e, ti=ti: e.tensor_tensor(
                dt[ti][:], qs[ti][:], ksrep[:], op=OP.mult),
                waits=((sS, st2["qc_exp"][ti]), st2["ksrep_dma"]))
            silent("vector", lambda e, ti=ti: e.tensor_reduce(
                den[ti][:].rearrange("p (h o) -> p h o", o=1),
                dt[ti][:].rearrange("p (h d) -> p h d", d=HD),
                axis=AX.X, op=OP.add))
        else:
            silent("vector", lambda e, ti=ti: e.tensor_scalar(
                den[ti][:], den[ti][:], EPS_DEN, None, op0=OP.add))
            silent("vector", lambda e, ti=ti: e.reciprocal(
                den[ti][:], den[ti][:]))
            e_qsc = ev("vector", sV, lambda e, ti=ti: e.tensor_tensor(
                qs[ti][:].rearrange("p (h d) -> p h d", d=HD),
                qs[ti][:].rearrange("p (h d) -> p h d", d=HD),
                den[ti][:].rearrange("p (h o) -> p h o", o=1).broadcast_to(
                    [128, H, HD]), op=OP.mult))
            st2["ksrep_free"] = e_qsc
            st2["qsc"][ti] = e_qsc

    def emit_x1tr(si, gi, tis=(0, 1)):
        par = gi % 2
        x1buf = x1[par]
        for ti in tis:
            for rr in range(2):
                bankT = p6 if rr == 0 else p7
                e_tr = 0
                fvk = "f6v" if rr == 0 else "f7v"
                for j in range(4):
                    c = rr * 4 + j
                    w = ()
                    if j == 0:
                        w = ((sS, st2["x1_ready"][2 * par + ti]),
                             (sV, st2[fvk]))
                    fn = (lambda e, c=c, j=j, x1buf=x1buf, ti=ti, bankT=bankT:
                          e.transpose(bankT[:, j * 128:(j + 1) * 128],
                                      x1buf[:, ti, c * 128:(c + 1) * 128],
                                      ident[:]))
                    if j == 3:
                        e_tr = ev("tensor", sT, fn, waits=w)
                    else:
                        silent("tensor", fn, waits=w)
                e_hi = ev("scalar", sS, lambda e, rr=rr, par=par, ti=ti,
                          bankT=bankT: e.activation(
                    x1Th[par][:, rr * 4:(rr + 1) * 4,
                              ti * 128:(ti + 1) * 128],
                    bankT[:, 0:512], AF.Identity),
                    waits=((sT, max(e_tr, st2["x1T_free"])),))
                e_lo = ev("vector", sV, lambda e, rr=rr, par=par, ti=ti,
                          bankT=bankT: e.scalar_tensor_tensor(
                    x1Tl[par][:, rr * 4:(rr + 1) * 4,
                              ti * 128:(ti + 1) * 128],
                    bankT[:, 0:512], 1.0,
                    x1Th[par][:, rr * 4:(rr + 1) * 4,
                              ti * 128:(ti + 1) * 128],
                    op0=OP.mult, op1=OP.subtract), waits=((sS, e_hi),))
                st2[fvk] = e_lo
                if ti == TG - 1:
                    st2["x1T_rr"][par][rr] = e_lo
                    st2["x1T_hi_rr"][par][rr] = e_hi
                    if rr == 1:
                        st2["x1T_ready"][par] = e_lo

    def emit_ln1_tile(x1buf, ti, par, e_r1):
        sq = 0.5
        ev("vector", sV, lambda e, x1buf=x1buf, ti=ti: e.tensor_reduce(
            lnt[:, ti:ti + 1], x1buf[:, ti, :], axis=AX.X, op=OP.add),
            waits=((sV, e_r1),))
        silent("vector", lambda e, ti=ti: e.tensor_scalar(
            lnt[:, 2 + ti:3 + ti], lnt[:, ti:ti + 1], 1.0 / D, None,
            op0=OP.mult))
        silent("vector", lambda e, ti=ti: e.tensor_scalar(
            lnt[:, 8 + ti:9 + ti], lnt[:, 2 + ti:3 + ti], -1.0, None,
            op0=OP.mult), waits=((sS, st2["lnt1_free"][ti]),))
        e_nmb = ev("vector", sV, lambda e, ti=ti: e.tensor_scalar(
            lnt[:, 14 + ti:15 + ti], lnt[:, 2 + ti:3 + ti], -(D ** -0.5),
            None, op0=OP.mult))
        e_sq = ev("scalar", sS, lambda e, x1buf=x1buf, ti=ti:
                  e.activation(
            qs[ti][:], x1buf[:, ti, :], AF.Square, scale=D ** -0.5,
            bias=lnt[:, 14 + ti:15 + ti],
            accum_out=lnt[:, 4 + ti:5 + ti]),
            waits=((sV, e_nmb), (sT, st2["qs_free"][ti])))
        st2["qs_sfree"][ti] = e_sq
        silent("scalar", lambda e, ti=ti: e.activation(
            lnt[:, 6 + ti:7 + ti], lnt[:, 4 + ti:5 + ti], AF.Ln,
            bias=lnt[:, 32:33]), waits=((sG, e_eps),))
        silent("scalar", lambda e, ti=ti: e.activation(
            lnt[:, 12 + ti:13 + ti], lnt[:, 6 + ti:7 + ti], AF.Exp,
            scale=-0.5))
        silent("scalar", lambda e, ti=ti: e.activation(
            lnt[:, 10 + ti:11 + ti], lnt[:, 8 + ti:9 + ti], AF.Identity,
            scale=lnt[:, 12 + ti:13 + ti]))
        e_n = ev("scalar", sS, lambda e, x1buf=x1buf, ti=ti: e.activation(
            x1buf[:, ti, :], x1buf[:, ti, :], AF.Identity,
            bias=lnt[:, 10 + ti:11 + ti], scale=lnt[:, 12 + ti:13 + ti]))
        st2["lnt1_free"][ti] = e_n
        st2["x1_ready"][2 * par + ti] = e_n

    def emit_attn_rest(si, gi):
        par = gi % 2
        e_yv = [0, 0]
        for ti in range(TG):
            e_qsT_r = [0, 0]
            for rr in range(2):
                bankQ = p4 if rr == 0 else p5
                fv = "f4v" if rr == 0 else "f5v"
                fs = "f4s" if rr == 0 else "f5s"
                e_tr = 0
                for j in range(4):
                    c = rr * 4 + j
                    w = ()
                    if j == 0:
                        w = ((sV, max(st2["qsc"][ti], st2[fv])),
                             (sS, st2[fs]))
                    fn = (lambda e, c=c, j=j, ti=ti, bankQ=bankQ: e.transpose(
                        bankQ[:, j * 128:(j + 1) * 128],
                        qs[ti][:, c * 128:(c + 1) * 128], ident[:]))
                    if j == 3:
                        e_tr = ev("tensor", sT, fn, waits=w)
                    else:
                        silent("tensor", fn, waits=w)
                if rr == 1:
                    st2["qs_free"][ti] = e_tr
                e_qsT_r[rr] = ev("scalar", sS, lambda e, rr=rr, ti=ti,
                                 bankQ=bankQ: e.activation(
                    qsT[ti][:, rr * 4:(rr + 1) * 4, :].rearrange(
                        "p a b -> p (a b)"), bankQ[:, 0:512], AF.Identity),
                    waits=((sT, max(e_tr, st2["qsT_free"][ti])),))
                st2[fs] = e_qsT_r[rr]
            e_yev_s = 0
            e_yev_v = 0
            for rr in range(2):
                bankY = p6 if rr == 0 else p7
                fv = "f6v" if rr == 0 else "f7v"
                e_ymm = 0
                for j in range(4):
                    pr = rr * 4 + j
                    w = ()
                    if j == 0:
                        w = ((sV, st2[fv]), (sG, e_kvz[si]),
                             (sS, e_qsT_r[rr]))
                    fn = (lambda e, pr=pr, j=j, ti=ti, si=si, bankY=bankY:
                          e.matmul(
                        bankY[:, j * 128:(j + 1) * 128],
                        kvacc[si][:, pr * 129:pr * 129 + 128],
                        qsT[ti][:, pr, :], start=True, stop=True))
                    if j == 3:
                        e_ymm = ev("tensor", sT, fn, waits=w)
                    else:
                        silent("tensor", fn, waits=w)
                if rr == 1:
                    st2["qsT_free"][ti] = e_ymm
                e_yev_s = ev("scalar", sS, lambda e, rr=rr, ti=ti, bankY=bankY:
                             e.activation(
                    yTh[ti][:, rr * 4:(rr + 1) * 4, :].rearrange(
                        "p a b -> p (a b)"), bankY[:, 0:512], AF.Identity),
                    waits=((sT, max(e_ymm, st2["yT_free"][ti])), (sS, e_yev_s)))
                e_yev_v = ev("vector", sV, lambda e, rr=rr, ti=ti, bankY=bankY:
                             e.scalar_tensor_tensor(
                    yTl[ti][:, rr * 4:(rr + 1) * 4, :].rearrange(
                        "p a b -> p (a b)"),
                    bankY[:, 0:512], 1.0,
                    yTh[ti][:, rr * 4:(rr + 1) * 4, :].rearrange(
                        "p a b -> p (a b)"),
                    op0=OP.mult, op1=OP.subtract), waits=((sS, e_yev_s),))
                st2[fv] = e_yev_v
            e_yv[ti] = e_yev_v
        x1buf = x1[par]
        e_r1 = 0
        for ti in range(TG):
            pa, pb_ = qpb[ti]
            e_op = 0
            for c in range(NP):
                w = (((sV, max(st2["qb_evac"][ti], e_yv[ti],
                               st2["h_evac"][0], st2["h_evac"][1])),
                      (sS, max(st2["h_hi"][0], st2["h_hi"][1])))
                     if c == 0 else ())
                st = (c == 0)
                sp = (c == NP - 1)
                silent("tensor", lambda e, c=c, st=st, ti=ti, pa=pa: e.matmul(
                    pa[:, 0:512], yTh[ti][:, c, :], wo_h[:, c, 0:512],
                    start=st, stop=False), waits=w)
                silent("tensor", lambda e, c=c, st=st, ti=ti, pb_=pb_: e.matmul(
                    pb_[:, 0:512], yTh[ti][:, c, :], wo_h[:, c, 512:1024],
                    start=st, stop=False))
                silent("tensor", lambda e, c=c, ti=ti, pa=pa: e.matmul(
                    pa[:, 0:512], yTh[ti][:, c, :], wo_l[:, c, 0:512],
                    start=False, stop=False))
                silent("tensor", lambda e, c=c, ti=ti, pb_=pb_: e.matmul(
                    pb_[:, 0:512], yTh[ti][:, c, :], wo_l[:, c, 512:1024],
                    start=False, stop=False))
                silent("tensor", lambda e, c=c, sp=sp, ti=ti, pa=pa: e.matmul(
                    pa[:, 0:512], yTl[ti][:, c, :], wo_h[:, c, 0:512],
                    start=False, stop=sp))
                if c == NP - 1:
                    e_op = ev("tensor", sT, lambda e, c=c, ti=ti, pb_=pb_:
                              e.matmul(pb_[:, 0:512], yTl[ti][:, c, :],
                                       wo_h[:, c, 512:1024],
                                       start=False, stop=True))
                else:
                    silent("tensor", lambda e, c=c, ti=ti, pb_=pb_: e.matmul(
                        pb_[:, 0:512], yTl[ti][:, c, :], wo_h[:, c, 512:1024],
                        start=False, stop=False))
            st2["yT_free"][ti] = e_op
            tb = ti % 2
            silent("vector", lambda e, pa=pa, x1buf=x1buf, ti=ti, tb=tb:
                   e.scalar_tensor_tensor(
                x1buf[:, ti, 0:512], pa[:, 0:512], 1.0, xt[tb][:, 0:512],
                op0=OP.mult, op1=OP.add),
                waits=((sT, e_op), (sV, st2["xt_rdy"][tb])))
            e_r1 = ev("vector", sV, lambda e, pb_=pb_, x1buf=x1buf, ti=ti,
                      tb=tb: e.scalar_tensor_tensor(
                x1buf[:, ti, 512:1024], pb_[:, 0:512], 1.0,
                xt[tb][:, 512:1024], op0=OP.mult, op1=OP.add))
            st2["r1_evac"][ti] = e_r1
            st2["xt_free"][tb] = e_r1
            emit_ln1_tile(x1buf, ti, par, e_r1)

    def emit_mm2(c, cb, wb, e_w2c, e_hev, is_first, is_last):
        e_mm2 = 0
        w0 = ((sV, max(e_hev,
                       st2["ffnout_evac"] if is_first else 0,
                       max(st2["f4v"], st2["f5v"], st2["f6v"],
                           st2["f7v"]) if is_first else 0)),
              ((sS, max(st2["f4s"], st2["f5s"])) if is_first else (sS, 0)),
              e_w2c)
        emitted = False
        for ti in range(TG):
            for hv in range(2):
                hbuf = hh[cb] if hv == 0 else hl[cb]
                for nci in range(2):
                    stf = (c == 0 and hv == 0)
                    spf = (c == F4C - 1 and hv == 1)
                    w = w0 if not emitted else ()
                    emitted = True
                    fn = (lambda e, wb=wb, ti=ti, hbuf=hbuf, nci=nci,
                          stf=stf, spf=spf: e.matmul(
                        PB[4 + 2 * ti + nci][:, 0:512],
                        hbuf[:, ti * 128:(ti + 1) * 128],
                        w2s[wb][:, nci * 512:(nci + 1) * 512],
                        start=stf, stop=spf))
                    if ti == TG - 1 and hv == 1 and nci == 1:
                        e_mm2 = ev("tensor", sT, fn, waits=w)
                    else:
                        silent("tensor", fn, waits=w)
        st2["mm2_ev"][wb] = e_mm2
        return e_mm2

    def emit_ffn(si, gi, inject=None):
        par = gi % 2
        e_mm1 = 0
        pend_mm2 = None
        e_w2c = [0, 0, 0]
        for c in range(F4C):
            if inject is not None:
                inject(c)
            if c == 16:
                for ti in range(TG):
                    silent("vector", lambda e, ti=ti, par=par: e.tensor_tensor(
                        x1[par][:, ti, :], x1[par][:, ti, :], b2_r[:],
                        op=OP.add),
                        waits=((sV, st2["x1T_ready"][par]), e_bias2))
            cb = c % 2
            wb = c % 3
            if c < 3 and st2["w_pre"] is not None:
                e_w1, e_w2c[wb] = st2["w_pre"][c]
            else:
                e_w1 = dma_in(sW1[wb], lambda e, c=c, wb=wb: e.dma_start(
                    out=w1s[wb][:], in_=w1q[c, :, :, :]),
                    waits=(e_scr_all[0], e_scr_all[1],
                           (sT, st2["w1_free"][wb])))
                e_w2c[wb] = dma_in(sW2[wb], lambda e, c=c, wb=wb: e.dma_start(
                    out=w2s[wb][:], in_=w2q[c, :, :]),
                    waits=((sT, st2["mm2_ev"][wb]),))
            hp = PB[cb]
            for fc in range(NP):
                w = ()
                if fc == 0:
                    w = (e_w1,
                         (sV, max(st2["h_evac"][cb],
                                  st2["qb_evac"][0] if c < 2 else 0)),
                         (sS, max(st2["h_hi"][cb],
                                  st2["x1T_hi_rr"][par][0])))
                elif fc == 4:
                    w = ((sS, st2["x1T_hi_rr"][par][1]),)
                silent("tensor", lambda e, c=c, fc=fc, wb=wb, par=par, hp=hp:
                       e.matmul(hp[:, 0:TG * 128], w1s[wb][:, fc, :],
                                x1Th[par][:, fc, :],
                                start=(fc == 0), stop=False), waits=w)
            for fc in range(NP):
                w = ()
                if fc == 0:
                    w = ((sV, st2["x1T_rr"][par][0]),)
                elif fc == 4:
                    w = ((sV, st2["x1T_rr"][par][1]),)
                if fc == NP - 1:
                    e_mm1 = ev("tensor", sT, lambda e, c=c, fc=fc, wb=wb,
                               par=par, hp=hp: e.matmul(
                        hp[:, 0:TG * 128], w1s[wb][:, fc, :],
                        x1Tl[par][:, fc, :], start=False, stop=True), waits=w)
                else:
                    silent("tensor", lambda e, c=c, fc=fc, wb=wb, par=par,
                           hp=hp: e.matmul(
                        hp[:, 0:TG * 128], w1s[wb][:, fc, :],
                        x1Tl[par][:, fc, :], start=False, stop=False),
                        waits=w)
            st2["w1_free"][wb] = e_mm1
            e_hi = ev("scalar", sS, lambda e, c=c, cb=cb, hp=hp: e.activation(
                hh[cb][:], hp[:, 0:TG * 128], AF.Relu, bias=b1_c[:, c:c + 1]),
                waits=((sT, e_mm1), e_b1c))
            e_h32 = ev("scalar", sS, lambda e, c=c, cb=cb, hp=hp: e.activation(
                h32[cb][:], hp[:, 0:TG * 128], AF.Relu, bias=b1_c[:, c:c + 1]))
            st2["h_hi"][cb] = e_h32
            st2["h_evac"][cb] = ev("vector", sV, lambda e, cb=cb:
                                   e.scalar_tensor_tensor(
                hl[cb][:], h32[cb][:], 1.0, hh[cb][:],
                op0=OP.mult, op1=OP.subtract), waits=((sS, e_h32),))
            if pend_mm2 is not None:
                emit_mm2(*pend_mm2)
            pend_mm2 = (c, cb, wb, e_w2c[wb], st2["h_evac"][cb], c == 0,
                        c == F4C - 1)
        e_mm2 = emit_mm2(*pend_mm2)
        e_r2_t = [0, 0]
        for ti in range(TG):
            x1buf = x1[par]
            silent("vector", lambda e, ti=ti, x1buf=x1buf:
                   e.scalar_tensor_tensor(
                ot2[:, ti, 0:512], PB[4 + 2 * ti][:, 0:512], 1.0,
                x1buf[:, ti, 0:512], op0=OP.mult, op1=OP.add),
                waits=((sT, e_mm2), st2["ot_free"][ti % 2]))
            e_r2 = ev("vector", sV, lambda e, ti=ti, x1buf=x1buf:
                      e.scalar_tensor_tensor(
                ot2[:, ti, 512:1024], PB[5 + 2 * ti][:, 0:512], 1.0,
                x1buf[:, ti, 512:1024], op0=OP.mult, op1=OP.add))
            e_r2_t[ti] = e_r2
        st2["ffnout_evac"] = e_r2_t[1]
        if not st2["last_ffn"]:
            pre = []
            for c in range(3):
                wb2 = c % 3
                ew1 = dma_out(sWP1[wb2], lambda e, c=c, wb2=wb2: e.dma_start(
                    out=w1s[wb2][:], in_=w1q[c, :, :, :]),
                    waits=((sT, st2["w1_free"][wb2]),))
                ew2 = dma_out(sWP2[wb2], lambda e, c=c, wb2=wb2: e.dma_start(
                    out=w2s[wb2][:], in_=w2q[c, :, :]),
                    waits=((sT, st2["mm2_ev"][wb2]),))
                pre.append((ew1, ew2))
            st2["w_pre"] = pre
        st2["f4v"] = max(st2["f4v"], e_r2_t[0])
        st2["f5v"] = max(st2["f5v"], e_r2_t[0])
        st2["f6v"] = max(st2["f6v"], e_r2_t[1])
        st2["f7v"] = max(st2["f7v"], e_r2_t[1])
        st2["r2_ev"] = e_r2_t

    LN2_SQ = 1.0 / 512

    def emit_ln2_seg4_store(psi, pgi, e_s3):
        for ti in range(TG):
            ob = ti % 2
            e_o = ev("scalar", sS, lambda e, ti=ti: e.activation(
                ot2[:, ti, :], ot2[:, ti, :], AF.Identity,
                bias=lnt[:, 26 + ti:27 + ti], scale=lnt[:, 28 + ti:29 + ti]),
                waits=((sV, e_s3),))
            st2["lnt2_free"] = e_o
            st2["ot_free"][ob] = dma_out(sOT[ob], lambda e, psi=psi, pgi=pgi,
                                         ti=ti: e.dma_start(
                out=out[psi, pgi * TG + ti, :, :], in_=ot2[:, ti, :]),
                waits=((sS, e_o),))

    def mk_ln2_injector(psi, pgi, e_r2_pair):
        state = {}

        def inj(c):
            if c == 6:
                emit_ln_red(ot2, 0, 16, e_r2_pair[0])
                emit_ln_red(ot2, 1, 16, e_r2_pair[1])
                state["s1"] = emit_ln_seg1b(LN2_SQ, 16, st2["lnt2_free"])
            elif c == 7:
                e_d = 0
                for ti in range(TG):
                    e_d = emit_ln_seg2(ot2, ti, LN2_SQ, 16, qs[1][:],
                                       state["s1"],
                                       ((sT, st2["qs_free"][1]),))
                st2["qs_sfree"][1] = e_d
                state["s2"] = e_d
            elif c == 9:
                state["s3"] = emit_ln_seg3(LN2_SQ, 16, state["s2"])
            elif c == 14:
                emit_ln2_seg4_store(psi, pgi, state["s3"])
        return inj

    pend = None
    pprev = None
    pprev_r2 = (0, 0)
    groups = [(si, gi) for si in range(NSEQ) for gi in range(p2_groups)]
    for idx, (si, gi) in enumerate(groups):
        if gi == 0:
            st2["ksrep_dma"] = dma_in(sKS, lambda e, si=si: e.dma_start(
                out=ksrep[:],
                in_=ksum_hbm[si, :][None, :].broadcast_to([128, D])),
                waits=(e_ksum_out[si], (sV, st2["ksrep_free"])))
        if idx == 0:
            load_xt(si, gi, 0)
            load_xt(si, gi, 1)
        emit_qproj(si, gi)
        if idx == 0:
            emit_seq_boundary(1)
            st2["f4v"] = ev_kvevac
            st2["f5v"] = ev_kvevac
            st2["f6v"] = ev_kvevac
            st2["f7v"] = e_kr_last
        if pend is not None:
            emit_x1tr(pend[0], pend[1], tis=(1,))
            inj2 = (mk_ln2_injector(pprev[0], pprev[1], pprev_r2)
                    if pprev is not None else None)

            def inject(c, si=si, gi=gi, inj2=inj2):
                emit_qchain_tile(si, gi, c)
                if inj2 is not None:
                    inj2(c)
            emit_ffn(pend[0], pend[1], inject=inject)
            pprev = pend
            pprev_r2 = tuple(st2["r2_ev"])
        else:
            for cc in (2, 3, 4, 10, 11, 12):
                emit_qchain_tile(si, gi, cc)
        if idx + 1 < len(groups):
            nsi, ngi = groups[idx + 1]
            load_x_tile(nsi, ngi * TG + 0)
            load_x_tile(nsi, ngi * TG + 1)
        emit_attn_rest(si, gi)
        if idx + 1 < len(groups):
            nsi, ngi = groups[idx + 1]
            load_xt(nsi, ngi, 0)
            load_xt(nsi, ngi, 1)
        emit_x1tr(si, gi, tis=(0,))
        pend = (si, gi)
    if pend is not None:
        emit_x1tr(pend[0], pend[1], tis=(1,))
        inj2 = (mk_ln2_injector(pprev[0], pprev[1], pprev_r2)
                if pprev is not None else None)
        st2["last_ffn"] = True
        emit_ffn(pend[0], pend[1],
                 inject=(lambda c, inj2=inj2: inj2(c)) if inj2 else None)
        emit_ln_red(ot2, 0, 16, st2["r2_ev"][0])
        emit_ln_red(ot2, 1, 16, st2["r2_ev"][1])
        e_s1 = emit_ln_seg1b(LN2_SQ, 16, st2["lnt2_free"])
        e_d = 0
        for ti in range(TG):
            e_d = emit_ln_seg2(ot2, ti, LN2_SQ, 16, qs[1][:], e_s1,
                               ((sT, st2["qs_free"][1]),))
        st2["qs_sfree"][1] = e_d
        e_s3 = emit_ln_seg3(LN2_SQ, 16, e_d)
        emit_ln2_seg4_store(pend[0], pend[1], e_s3)
    silent("gpsimd", lambda e: e.nop(),
           waits=((sOT[0], sOT[0].n), (sOT[1], sOT[1].n),
                  (sQW[0], sQW[0].n), (sQW[1], sQW[1].n),
                  (sKSO, sKSO.n), (sGAM, sGAM.n)))

    with nc.Block() as block:
        @block.sync
        def _(e):
            sch.replay("sync", e)

        @block.tensor
        def _(e):
            sch.replay("tensor", e)

        @block.vector
        def _(e):
            sch.replay("vector", e)

        @block.scalar
        def _(e):
            sch.replay("scalar", e)

        @block.gpsimd
        def _(e):
            sch.replay("gpsimd", e)

    for g in reversed(alloc):
        g.__exit__(None, None, None)
    return nc


def stage_inputs(L0_attn_x, L0_attn_Wq, L0_attn_bq, L0_attn_Wk, L0_attn_bk,
                 L0_attn_Wv, L0_attn_bv, L0_attn_Wo, L0_attn_bo,
                 L0_ffn_W1, L0_ffn_b1, L0_ffn_W2, L0_ffn_b2):
    x = np.asarray(L0_attn_x, dtype=np.float32)
    shared = {
        "wqT": np.ascontiguousarray(np.asarray(L0_attn_Wq).T).reshape(NP, 128, D),
        "wkT": np.ascontiguousarray(np.asarray(L0_attn_Wk).T).reshape(NP, 128, D),
        "wvT": np.ascontiguousarray(np.asarray(L0_attn_Wv).T).reshape(NP, 128, D),
        "woT": np.ascontiguousarray(np.asarray(L0_attn_Wo).T).reshape(NP, 128, D),
        "w1r": np.ascontiguousarray(
            np.asarray(L0_ffn_W1).T.reshape(NP, 128, F4C, 128).transpose(
                2, 1, 0, 3)),
        "w2r": np.ascontiguousarray(np.asarray(L0_ffn_W2).T).reshape(
            F4C, 128, D),
        "bqd": np.ascontiguousarray(L0_attn_bq),
        "bkd": np.ascontiguousarray(L0_attn_bk),
        "bvd": np.ascontiguousarray(L0_attn_bv),
        "bod": np.ascontiguousarray(L0_attn_bo),
        "b1d": np.ascontiguousarray(np.asarray(L0_ffn_b1).reshape(F4C, 128).T),
        "b2d": np.ascontiguousarray(L0_ffn_b2),
    }
    in_maps = []
    for c in range(NCORES):
        xs = x[2 * c:2 * c + 2]
        m = dict(shared)
        m["xtm"] = np.ascontiguousarray(xs).reshape(NSEQ, NT, 128, D)
        m["xfm"] = np.ascontiguousarray(xs.transpose(0, 2, 1)).reshape(
            NSEQ, NP, 128, S)
        in_maps.append(m)
    return in_maps


def kernel(**inputs):
    in_maps = stage_inputs(**inputs)
    nc = build_program()
    res = run_bass_kernel_spmd(nc, in_maps, core_ids=list(range(NCORES)))
    parts = [np.asarray(r["out"]).reshape(NSEQ * S, D) for r in res.results]
    return np.concatenate(parts, axis=0).reshape(16, S, D)


# version 44
# speedup vs baseline: 1.0812x; 1.0036x over previous
import math
import numpy as np
import concourse.bass as bass
import concourse.mybir as mybir
from concourse.bass_utils import run_bass_kernel_spmd
from concourse.alu_op_type import AluOpType

F32 = mybir.dt.float32
BF16 = mybir.dt.bfloat16
AF = mybir.ActivationFunctionType
AX = mybir.AxisListType
OP = AluOpType

NCORES = 8
NSEQ = 2
S = 2048
D = 1024
H = 16
HD = 64
NP = 8
F4 = 4096
F4C = 32
NT = 16
TG = 2
NG = NT // TG
EPS_LN = 1e-5
EPS_Q = 1e-5
EPS_DEN = 1e-6
NW1 = F4 * D
NW2 = D * F4


class Sched:

    def __init__(self):
        self.ops = {k: [] for k in ("sync", "tensor", "vector", "scalar", "gpsimd")}

    def op(self, eng, fn, waits=(), inc=None, inc_n=1, self_wait=None):
        self.ops[eng].append((fn, tuple(waits), inc, inc_n, self_wait))

    def replay(self, name, e):
        seen = {}
        for fn, waits, inc, inc_n, self_wait in self.ops[name]:
            allw = waits if self_wait is None else (waits + (self_wait,))
            for sem, val in allw:
                if val <= 0 or seen.get(id(sem), -1) >= val:
                    continue
                e.wait_ge(sem.h, val)
                seen[id(sem)] = val
            ins = fn(e)
            if inc is not None:
                ins.then_inc(inc.h, inc_n)


class Sem:
    def __init__(self, h):
        self.h = h
        self.n = 0


def build_program(p2_groups=NG):
    nc = bass.Bass("TRN2", target_bir_lowering=False)

    xfm = nc.dram_tensor("xfm", [NSEQ, NT, 128, NP * 128], F32, kind="ExternalInput").ap()
    xtm = nc.dram_tensor("xtm", [NSEQ, NT, 128, D], F32, kind="ExternalInput").ap()
    wqT = nc.dram_tensor("wqT", [NP, 128, D], F32, kind="ExternalInput").ap()
    wkT = nc.dram_tensor("wkT", [NP, 128, D], F32, kind="ExternalInput").ap()
    wvT = nc.dram_tensor("wvT", [NP, 128, D], F32, kind="ExternalInput").ap()
    woT = nc.dram_tensor("woT", [NP, 128, D], F32, kind="ExternalInput").ap()
    w1r = nc.dram_tensor("w1r", [F4C, 128, NP, 128], F32, kind="ExternalInput").ap()
    w2r = nc.dram_tensor("w2r", [F4C, 128, D], F32, kind="ExternalInput").ap()
    bqd = nc.dram_tensor("bqd", [D], F32, kind="ExternalInput").ap()
    bkd = nc.dram_tensor("bkd", [D], F32, kind="ExternalInput").ap()
    bvd = nc.dram_tensor("bvd", [D], F32, kind="ExternalInput").ap()
    bod = nc.dram_tensor("bod", [D], F32, kind="ExternalInput").ap()
    b1d = nc.dram_tensor("b1d", [128, F4C], F32, kind="ExternalInput").ap()
    b2d = nc.dram_tensor("b2d", [D], F32, kind="ExternalInput").ap()
    out = nc.dram_tensor("out", [NSEQ, NT, 128, D], F32, kind="ExternalOutput").ap()
    w1q = nc.dram_tensor("w1q", [F4C, 128, NP, 128], BF16, kind="Internal").ap()
    w2q = nc.dram_tensor("w2q", [F4C, 128, D], BF16, kind="Internal").ap()
    ksum_hbm = nc.dram_tensor("ksum_hbm", [NSEQ, D], F32, kind="Internal").ap()
    gam_hbm = nc.dram_tensor("gam_hbm", [2, 1], F32, kind="Internal").ap()

    sch = Sched()
    alloc = []

    def sb(name, shape, dt):
        g = nc.sbuf_tensor(name, shape, dt)
        t = g.__enter__()
        alloc.append(g)
        return t

    wq_h = sb("wq_h", [128, NP, D], BF16)
    wq_l = sb("wq_l", [128, NP, D], BF16)
    wo_h = sb("wo_h", [128, NP, D], BF16)
    wo_l = sb("wo_l", [128, NP, D], BF16)
    xg = [sb("xg0", [128, NP, 128], F32), sb("xg1", [128, NP, 128], F32)]
    xgh = [sb(f"xgh{i}", [128, NP, 128], BF16) for i in range(4)]
    xgl = [sb(f"xgl{i}", [128, NP, 128], BF16) for i in range(4)]
    kvacc = [sb("kv0", [128, 8 * 129], F32), sb("kv1", [128, 8 * 129], F32)]
    ident = sb("ident", [128, 128], F32)
    ksrow = sb("ksrow", [8, 128], F32)
    b1_c = sb("b1_c", [128, F4C], F32)
    gsum2 = sb("gsum2", [2, 1], F32)

    p1_guards = []

    def sb1(name, shape, dt):
        g = nc.sbuf_tensor(name, shape, dt)
        t = g.__enter__()
        p1_guards.append(g)
        return t

    wk_h = sb1("wk_h", [128, NP, D], BF16)
    wk_l = sb1("wk_l", [128, NP, D], BF16)
    wv_h = sb1("wv_h", [128, NP, D], BF16)
    wv_l = sb1("wv_l", [128, NP, D], BF16)
    wrc = [sb1("wrc0", [128, D], F32), sb1("wrc1", [128, D], F32)]
    bk_r = sb1("bk_r", [128, D], F32)
    bv_r = sb1("bv_r", [128, D], F32)
    kkern = [sb1("kkern0", [128, D], F32), sb1("kkern1", [128, D], F32)]
    vprime = [sb1("vprime0", [128, 8 * 129], F32),
              sb1("vprime1", [128, 8 * 129], F32)]
    scr0 = sb1("scr0", [128, D], BF16)
    nm1 = sb1("nm1", [128, H], F32)
    wsl = [sb1("wsl0", [128, NP * 128], F32), sb1("wsl1", [128, NP * 128], F32)]
    wslq = [sb1("wslq0", [128, NP * 128], BF16),
            sb1("wslq1", [128, NP * 128], BF16)]
    absps = sb1("absps", [128, 72], F32)

    def ps(name):
        g = nc.psum_tensor(name, [128, 512], F32)
        t = g.__enter__()
        alloc.append(g)
        return t

    PB = [ps(f"p{i}") for i in range(8)]
    p0, p1, p2, p3, p4, p5, p6, p7 = PB

    def mksem(name):
        g = nc.semaphore(name)
        h = g.__enter__()
        alloc.append(g)
        return Sem(h)

    sT = mksem("sT")
    sV = mksem("sV")
    sS = mksem("sS")
    sG = mksem("sG")
    sB = mksem("sB")
    sXG = [mksem("sXG0"), mksem("sXG1")]
    sXT = [mksem("sXT0"), mksem("sXT1")]
    sWRC = [mksem("sWRC0"), mksem("sWRC1")]
    sWSL = [mksem("sWSL0"), mksem("sWSL1")]
    sWG = [mksem("sWG0"), mksem("sWG1")]
    sWP1 = [mksem(f"sWP1{i}") for i in range(3)]
    sWP2 = [mksem(f"sWP2{i}") for i in range(3)]
    sW1 = [mksem("sW10"), mksem("sW11"), mksem("sW12")]
    sW2 = [mksem("sW20"), mksem("sW21"), mksem("sW22")]
    sKS = mksem("sKS")
    sOT = [mksem("sOT0"), mksem("sOT1")]
    sQW = [mksem("sQW0"), mksem("sQW1")]
    sKSO = mksem("sKSO")
    sGAM = mksem("sGAM")

    def dma_in(sem, fn, waits=()):
        sch.op("sync", fn, waits=waits, inc=sem, inc_n=16)
        sem.n += 16
        return (sem, sem.n)

    def dma_out(sem, fn, waits=()):
        sch.op("gpsimd", fn, waits=waits, inc=sem, inc_n=16,
               self_wait=(sG, sG.n))
        sem.n += 16
        return (sem, sem.n)

    ENG_SEM = {"vector": sV, "scalar": sS, "gpsimd": sG}

    def ev(eng, sem, fn, waits=()):
        es = ENG_SEM.get(eng)
        if es is not None:
            assert sem is es, f"{eng} ops must inc their engine sem"
            sw = (es, es.n)
            sch.op(eng, fn, waits=waits, inc=sem, inc_n=1, self_wait=sw)
        else:
            sch.op(eng, fn, waits=waits, inc=sem, inc_n=1)
        sem.n += 1
        return sem.n

    def silent(eng, fn, waits=()):
        es = ENG_SEM.get(eng)
        if es is not None:
            sw = (es, es.n)
            sch.op(eng, fn, waits=waits, inc=es, inc_n=1, self_wait=sw)
            es.n += 1
        else:
            sch.op(eng, fn, waits=waits)

    eB = (sB, 0)
    for bsrc, bdst in ((bkd, bk_r), (bvd, bv_r)):
        eB = dma_in(sB, lambda e, s=bsrc, d=bdst: e.dma_start(
            out=d[:], in_=s[None, :].broadcast_to([128, D])), waits=(eB,))
    e_bias1 = eB
    eB = dma_in(sB, lambda e: e.dma_start(out=b1_c[:], in_=b1d[:, :]),
                waits=(eB,))
    e_b1c = eB

    silent("gpsimd", lambda e: e.memset(ident[:], 0.0))
    silent("gpsimd", lambda e: e.affine_select(
        out=ident[:], in_=ident[:], compare_op=OP.not_equal, fill=1.0,
        base=0, pattern=[[-1, 128]], channel_multiplier=1))
    silent("gpsimd", lambda e: e.memset(
        vprime[0][:].rearrange("p (a b) -> p a b", b=129)[:, :, 128:129], 1.0))
    e_gconst = ev("gpsimd", sG, lambda e: e.memset(
        vprime[1][:].rearrange("p (a b) -> p a b", b=129)[:, :, 128:129], 1.0))

    e_wrc_v = [0, 0]
    qslab_v = [0, 0]
    e_wrc_s = [0, 0]
    wg_state = {"k": 0}

    def split_chunk_sync(src, dsth, dstl, c):
        cb = c % 2
        e_d = dma_in(sWRC[cb], lambda e, src=src, c=c, cb=cb: e.dma_start(
            out=wrc[cb][:], in_=src[c, :, :]),
            waits=((sV, e_wrc_v[cb]), (sS, e_wrc_s[cb])))
        e_wrc_s[cb] = ev("scalar", sS, lambda e, c=c, cb=cb, dsth=dsth:
                         e.activation(
            dsth[:, c, :], wrc[cb][:], AF.Identity), waits=(e_d,))
        e_l = ev("vector", sV, lambda e, c=c, cb=cb, dsth=dsth, dstl=dstl:
                 e.scalar_tensor_tensor(
            dstl[:, c, :], wrc[cb][:], 1.0, dsth[:, c, :],
            op0=OP.mult, op1=OP.subtract), waits=((sS, e_wrc_s[cb]),))
        e_wrc_v[cb] = e_l
        return e_l

    def split_chunk_g(src, dsth, dstl, c):
        b = wg_state["k"] % 2
        wg_state["k"] += 1
        e_d = dma_out(sWG[b], lambda e, src=src, c=c, b=b: e.dma_start(
            out=wsl[b][:], in_=src[c, :, :]), waits=((sV, qslab_v[b]),))
        e_h = ev("scalar", sS, lambda e, c=c, b=b, dsth=dsth: e.activation(
            dsth[:, c, :], wsl[b][:], AF.Identity), waits=(e_d,))
        e_l = ev("vector", sV, lambda e, c=c, b=b, dsth=dsth, dstl=dstl:
                 e.scalar_tensor_tensor(
            dstl[:, c, :], wsl[b][:], 1.0, dsth[:, c, :],
            op0=OP.mult, op1=OP.subtract), waits=((sS, e_h),))
        qslab_v[b] = e_l
        return e_l

    xg_state = {"dma": [(sXG[0], 0), (sXG[1], 0)], "split": [0, 0, 0, 0],
                "splitb2": [0, 0], "tfree": [0, 0, 0, 0]}

    def load_x_tile(si, i):
        bp = i % 2
        p4 = i % 4
        t0 = i * 128
        e_d = dma_in(sXG[bp], lambda e, si=si, i=i, bp=bp: e.dma_start(
            out=xg[bp][:].rearrange("p c t -> p (c t)"),
            in_=xfm[si, i, :, :]),
            waits=((sV, xg_state["splitb2"][bp]),))
        e_h = ev("scalar", sS, lambda e, bp=bp, p4=p4: e.activation(
            xgh[p4][:].rearrange("p c t -> p (c t)"),
            xg[bp][:].rearrange("p c t -> p (c t)"), AF.Identity),
            waits=(e_d, (sT, xg_state["tfree"][p4])))
        e_s = ev("vector", sV, lambda e, bp=bp, p4=p4: e.scalar_tensor_tensor(
            xgl[p4][:].rearrange("p c t -> p (c t)"),
            xg[bp][:].rearrange("p c t -> p (c t)"), 1.0,
            xgh[p4][:].rearrange("p c t -> p (c t)"),
            op0=OP.mult, op1=OP.subtract), waits=((sS, e_h),))
        xg_state["dma"][bp] = e_d
        xg_state["split"][p4] = e_s
        xg_state["splitb2"][bp] = e_s
        return e_s

    def proj3(bp, wh, wl, pa, pb_, first_waits, set_tfree=False,
              per_chunk=None):
        last = 0
        for c in range(NP):
            w = first_waits if c == 0 else ()
            if per_chunk is not None:
                w = w + ((sV, per_chunk[c]),)
            st = (c == 0)
            sp = (c == NP - 1)
            silent("tensor", lambda e, c=c, st=st, bp=bp, wh=wh, pa=pa: e.matmul(
                pa[:, 0:512], xgh[bp][:, c, :], wh[:, c, 0:512],
                start=st, stop=False), waits=w)
            silent("tensor", lambda e, c=c, st=st, bp=bp, wh=wh, pb_=pb_: e.matmul(
                pb_[:, 0:512], xgh[bp][:, c, :], wh[:, c, 512:1024],
                start=st, stop=False))
            silent("tensor", lambda e, c=c, bp=bp, wl=wl, pa=pa: e.matmul(
                pa[:, 0:512], xgh[bp][:, c, :], wl[:, c, 0:512],
                start=False, stop=False))
            silent("tensor", lambda e, c=c, bp=bp, wl=wl, pb_=pb_: e.matmul(
                pb_[:, 0:512], xgh[bp][:, c, :], wl[:, c, 512:1024],
                start=False, stop=False))
            silent("tensor", lambda e, c=c, sp=sp, bp=bp, wh=wh, pa=pa: e.matmul(
                pa[:, 0:512], xgl[bp][:, c, :], wh[:, c, 0:512],
                start=False, stop=sp))
            if c == NP - 1:
                last = ev("tensor", sT, lambda e, c=c, bp=bp, wh=wh, pb_=pb_:
                          e.matmul(pb_[:, 0:512], xgl[bp][:, c, :],
                                   wh[:, c, 512:1024], start=False, stop=True))
            else:
                silent("tensor", lambda e, c=c, bp=bp, wh=wh, pb_=pb_: e.matmul(
                    pb_[:, 0:512], xgl[bp][:, c, :], wh[:, c, 512:1024],
                    start=False, stop=False))
        if set_tfree:
            xg_state["tfree"][bp] = last
        return last

    qst = {"gi": 0, "qi": 0, "e_thr": 0, "e_gam_p7": 0, "g_used": 0,
           "slab_s": [0, 0],
           "slabq_out": [(sQW[0], 0), (sQW[1], 0)]}

    def q_src(i):
        if i < 32:
            return w1r[i, :, :, :].rearrange("p c d -> p (c d)")
        return w2r[i - 32, :, :]

    def emit_gamma_slab():
        i = qst["gi"]
        if i >= 64:
            return
        qst["gi"] += 1
        b = i % 2
        e_d = dma_in(sWSL[b], lambda e, i=i, b=b: e.dma_start(
            out=wsl[b][:], in_=q_src(i)), waits=((sV, qslab_v[b]),))
        qslab_v[b] = ev("vector", sV, lambda e, i=i, b=b: e.tensor_reduce(
            absps[:, i:i + 1], wsl[b][:], axis=AX.X, op=OP.add,
            apply_absolute_value=True), waits=(e_d,))

    def emit_gamma_finalize():
        nonlocal eB
        ev("vector", sV, lambda e: e.tensor_reduce(
            absps[:, 64:65], absps[:, 0:32], axis=AX.X, op=OP.add),
            waits=((sV, max(qslab_v)),))
        e2 = ev("vector", sV, lambda e: e.tensor_reduce(
            absps[:, 65:66], absps[:, 32:64], axis=AX.X, op=OP.add))
        e_t = ev("tensor", sT, lambda e: e.transpose(
            p7[0:2, 0:128], absps[:, 64:66], ident[:]),
            waits=((sV, e2), (sG, e_gconst)))
        e3 = ev("vector", sV, lambda e: e.tensor_reduce(
            gsum2[:, 0:1], p7[0:2, 0:128], axis=AX.X, op=OP.add),
            waits=((sT, e_t),))
        qst["e_gam_p7"] = e3
        e_g = dma_out(sGAM, lambda e: e.dma_start(
            out=gam_hbm[:, :], in_=gsum2[:, :]), waits=((sV, e3),))
        eB = dma_in(sB, lambda e: e.dma_start(
            out=absps[:, 66:67],
            in_=gam_hbm[0, :][None, :].broadcast_to([128, 1])),
            waits=(eB, e_g))
        eB = dma_in(sB, lambda e: e.dma_start(
            out=absps[:, 67:68],
            in_=gam_hbm[1, :][None, :].broadcast_to([128, 1])), waits=(eB,))
        silent("vector", lambda e: e.tensor_scalar(
            absps[:, 68:69], absps[:, 66:67], 0.5 / NW1, 0.5 * EPS_Q,
            op0=OP.mult, op1=OP.add), waits=(eB,))
        silent("vector", lambda e: e.tensor_scalar(
            absps[:, 69:70], absps[:, 68:69], -1.0, None, op0=OP.mult))
        silent("vector", lambda e: e.tensor_scalar(
            absps[:, 70:71], absps[:, 67:68], 0.5 / NW2, 0.5 * EPS_Q,
            op0=OP.mult, op1=OP.add))
        qst["e_thr"] = ev("vector", sV, lambda e: e.tensor_scalar(
            absps[:, 71:72], absps[:, 70:71], -1.0, None, op0=OP.mult))

    def emit_quant_slab():
        i = qst["qi"]
        if i >= 64:
            return
        qst["qi"] += 1
        b = i % 2
        if i < 32:
            dst = w1q[i, :, :, :].rearrange("p c d -> p (c d)")
            tpos, tneg = 68, 69
        else:
            dst = w2q[i - 32, :, :]
            tpos, tneg = 70, 71
        e_d = dma_in(sWSL[b], lambda e, i=i, b=b: e.dma_start(
            out=wsl[b][:], in_=q_src(i)), waits=((sV, qslab_v[b]),))
        silent("vector", lambda e, b=b, tneg=tneg: e.tensor_scalar(
            scr0[:], wsl[b][:], absps[:, tneg:tneg + 1], None, op0=OP.is_lt),
            waits=(e_d, (sV, qst["e_thr"]), qst["slabq_out"][b]))
        qslab_v[b] = ev("vector", sV, lambda e, b=b, tpos=tpos:
                        e.scalar_tensor_tensor(
            wslq[b][:], wsl[b][:], absps[:, tpos:tpos + 1], scr0[:],
            op0=OP.is_gt, op1=OP.subtract))
        qst["slabq_out"][b] = dma_out(
            sQW[b], lambda e, b=b, d=dst: e.dma_start(out=d, in_=wslq[b][:]),
            waits=((sV, qslab_v[b]),))

    ev_kevac = 0
    ev_vevac = 0
    ev_kk = 0
    ev_kvevac = 0
    e_kvmm = 0
    e_kvmm_kb = [0, 0]
    e_ksum_out = [(sKSO, 0), (sKSO, 0)]
    e_kvz = [0, 0]
    e_kr_last = 0

    load_x_tile(0, 0)
    wk_evs = []
    wv_evs = []
    for c in range(NP):
        wk_evs.append(split_chunk_sync(wkT, wk_h, wk_l, c))
        wv_evs.append(split_chunk_g(wvT, wv_h, wv_l, c))
    e_wkv = wk_evs[-1]
    wqo_pend = [(wqT, wq_h, wq_l, c) for c in range(NP)] + \
               [(woT, wo_h, wo_l, c) for c in range(NP)]
    pend_kv = None

    def emit_kv(i, kb, e_kk, e_vev):
        nonlocal e_kvmm
        for pr in range(8):
            bank = PB[4 + pr // 3]
            off = (pr % 3) * 129
            w = (((sS, e_kk), (sV, max(e_vev, ev_kvevac)))
                 if pr == 0 else ())
            fn = (lambda e, pr=pr, bank=bank, off=off, i=i, kb=kb: e.matmul(
                bank[:, off:off + 129],
                kkern[kb][:, pr * 128:(pr + 1) * 128],
                vprime[kb][:, pr * 129:(pr + 1) * 129],
                start=(i == 0 and pr % 3 == 0),
                stop=(i == NT - 1 and (pr % 3 == 2 or pr == 7))))
            if pr == 7:
                e_kvmm = ev("tensor", sT, fn, waits=w)
                e_kvmm_kb[kb] = e_kvmm
            else:
                silent("tensor", fn, waits=w)

    def emit_seq_boundary(si):
        nonlocal ev_kvevac, e_kr_last
        kva = kvacc[si]
        silent("vector", lambda e, kva=kva: e.tensor_copy(
            kva[:, 0:387], p4[:, 0:387]), waits=((sT, e_kvmm),))
        silent("vector", lambda e, kva=kva: e.tensor_copy(
            kva[:, 387:774], p5[:, 0:387]))
        ev_kvevac = ev("vector", sV, lambda e, kva=kva: e.tensor_copy(
            kva[:, 774:1032], p6[:, 0:258]))
        for pr in range(8):
            col = pr * 129
            silent("gpsimd", lambda e, kva=kva, col=col: e.memset(
                kva[0:64, col + 64:col + 128], 0.0), waits=((sV, ev_kvevac),))
            if pr == 7:
                e_kvz[si] = ev("gpsimd", sG, lambda e, kva=kva, col=col:
                               e.memset(kva[64:128, col:col + 64], 0.0))
            else:
                silent("gpsimd", lambda e, kva=kva, col=col: e.memset(
                    kva[64:128, col:col + 64], 0.0))
        e_t = ev("tensor", sT, lambda e, kva=kva: e.transpose(
            p7[0:8, 0:128],
            kva[:].rearrange("p (a b) -> p a b", b=129)[:, :, 128],
            ident[:]), waits=((sV, max(ev_kvevac, qst["e_gam_p7"])),))
        e_kr_last = ev("vector", sV, lambda e: e.tensor_copy(
            ksrow[:, :], p7[0:8, 0:128]),
            waits=((sT, e_t), (sKSO, 16 * si)))
        e_ksum_out[si] = dma_out(sKSO, lambda e, si=si: e.dma_start(
            out=ksum_hbm[si, :].rearrange("(a b) -> a b", a=8),
            in_=ksrow[:, :]),
            waits=((sV, e_kr_last), (sKSO, 16 * si)))

    for si in range(NSEQ):
        for i in range(NT):
            bp = i % 4
            if i < NT - 1:
                load_x_tile(si, i + 1)
            elif si == 0:
                load_x_tile(1, 0)
            if si == 0:
                for _ in range(2):
                    if wqo_pend:
                        split_chunk_g(*wqo_pend.pop(0))
                for _ in range(4):
                    emit_gamma_slab()
            elif not (si == 1 and i == 0):
                for _ in range(5 if i in (1, 2, 3) else 4):
                    emit_quant_slab()
            first_tile = (si == 0 and i == 0)
            e_kp = proj3(bp, wk_h, wk_l, p0, p1,
                         (e_bias1,
                          (sV, max(xg_state["split"][bp], ev_kevac, e_wkv))),
                         set_tfree=False,
                         per_chunk=wk_evs if first_tile else None)
            e_vpmm = proj3(bp, wv_h, wv_l, p2, p3, ((sV, ev_vevac),),
                           set_tfree=True,
                           per_chunk=wv_evs if first_tile else None)
            if pend_kv is not None:
                emit_kv(*pend_kv)
                pend_kv = None
            kb = i % 2
            silent("vector", lambda e, kb=kb: e.scalar_tensor_tensor(
                kkern[kb][:, 0:512], p0[:, 0:512], 1.0, bk_r[:, 0:512],
                op0=OP.mult, op1=OP.add),
                waits=((sT, max(e_kp, e_kvmm_kb[kb])),))
            ev_kevac = ev("vector", sV, lambda e, kb=kb: e.scalar_tensor_tensor(
                kkern[kb][:, 512:1024], p1[:, 0:512], 1.0, bk_r[:, 512:1024],
                op0=OP.mult, op1=OP.add))
            silent("vector", lambda e, kb=kb: e.tensor_reduce(
                nm1[:].rearrange("p (h o) -> p h o", o=1),
                kkern[kb][:].rearrange("p (h d) -> p h d", d=HD),
                axis=AX.X, op=OP.max, negate=True))
            e_sub = ev("vector", sV, lambda e, kb=kb: e.tensor_tensor(
                kkern[kb][:].rearrange("p (h d) -> p h d", d=HD),
                kkern[kb][:].rearrange("p (h d) -> p h d", d=HD),
                nm1[:].rearrange("p (h o) -> p h o", o=1).broadcast_to(
                    [128, H, HD]), op=OP.add))
            ev_kk = ev("scalar", sS, lambda e, kb=kb: e.activation(
                kkern[kb][:], kkern[kb][:], AF.Exp), waits=((sV, e_sub),))
            silent("vector", lambda e, kb=kb: e.scalar_tensor_tensor(
                vprime[kb][:].rearrange("p (a b) -> p a b", b=129)
                [:, 0:4, 0:128],
                p2[:, 0:512], 1.0, bv_r[:, 0:512], op0=OP.mult, op1=OP.add),
                waits=((sT, max(e_vpmm, e_kvmm_kb[kb])), (sG, e_gconst)))
            ev_vevac = ev("vector", sV, lambda e, kb=kb: e.scalar_tensor_tensor(
                vprime[kb][:].rearrange("p (a b) -> p a b", b=129)
                [:, 4:8, 0:128],
                p3[:, 0:512], 1.0, bv_r[:, 512:1024], op0=OP.mult, op1=OP.add))
            pend_kv = (i, kb, ev_kk, ev_vevac)
            if si == 0 and i == NT - 1:
                emit_gamma_finalize()
            if si == 1 and i == 0:
                emit_seq_boundary(0)
                emit_quant_slab()
        if pend_kv is not None:
            emit_kv(*pend_kv)
            pend_kv = None

    while qst["qi"] < 64:
        emit_quant_slab()
    e_scr_all = (qst["slabq_out"][0], qst["slabq_out"][1])
    eT1 = sT.n
    eV1 = sV.n
    eS1 = sS.n
    eG1 = sG.n

    for g in reversed(p1_guards):
        g.__exit__(None, None, None)

    bq_r = sb("bq_r", [128, D], F32)
    bo_r = sb("bo_r", [128, D], F32)
    b2_r = sb("b2_r", [128, D], F32)
    ksrep = sb("ksrep", [128, D], F32)
    xt = [sb("xt0", [128, D], F32), sb("xt1", [128, D], F32)]
    qb = [sb("qb0", [128, D], F32), sb("qb1", [128, D], F32)]
    qs = [sb("qs0", [128, D], F32), sb("qs1", [128, D], F32)]
    qsT = [qb[0].rearrange("p (a b) -> p a b", b=128),
           qb[1].rearrange("p (a b) -> p a b", b=128)]
    dt = qb
    nm = [sb("nm0", [128, H], F32), sb("nm1b", [128, H], F32)]
    den = [sb("den0", [128, H], F32), sb("den1", [128, H], F32)]
    yTh = [sb("yTh0", [128, NP, 128], BF16), sb("yTh1", [128, NP, 128], BF16)]
    yTl = [sb("yTl0", [128, NP, 128], BF16), sb("yTl1", [128, NP, 128], BF16)]
    x1 = [sb("x10", [128, TG, D], F32), sb("x11", [128, TG, D], F32)]
    x1Th = [sb("x1Th0", [128, NP, TG * 128], BF16),
            sb("x1Th1", [128, NP, TG * 128], BF16)]
    x1Tl = [sb("x1Tl0", [128, NP, TG * 128], BF16),
            sb("x1Tl1", [128, NP, TG * 128], BF16)]
    w1s = [sb(f"w1s{i}", [128, NP, 128], BF16) for i in range(3)]
    w2s = [sb(f"w2s{i}", [128, D], BF16) for i in range(3)]
    hh = [sb("hh0", [128, TG * 128], BF16), sb("hh1", [128, TG * 128], BF16)]
    hl = [sb("hl0", [128, TG * 128], BF16), sb("hl1", [128, TG * 128], BF16)]
    h32 = [sb("h320", [128, TG * 128], F32), sb("h321", [128, TG * 128], F32)]
    lnt = sb("lnt", [128, 36], F32)
    ot2 = sb("ot2", [128, TG, D], F32)

    e_eps = ev("gpsimd", sG, lambda e: e.memset(lnt[:, 32:33], EPS_LN),
               waits=((sT, eT1), (sV, eV1), (sS, eS1)))

    st2 = {
        "r1_evac": [ev_kevac, ev_vevac],
        "qb_evac": [0, 0],
        "qb_free": [eS1, eS1],
        "ffnout_evac": 0,
        "h_evac": [0, 0],
        "h_hi": [0, 0],
        "w1_free": [eT1, eT1, eT1],
        "e_mm1_last": eT1,
        "x1T_ready": [0, 0],
        "x1T_rr": [[0, 0], [0, 0]],
        "x1T_hi_rr": [[0, 0], [0, 0]],
        "x1_ready": [0, 0, 0, 0],
        "x1T_free": eT1,
        "xt_rdy": [0, 0],
        "xt_free": [0, 0],
        "ot_free": [(sOT[0], 0), (sOT[1], 0)],
        "ksrep_dma": (sKS, 0),
        "ksrep_free": 0,
        "f4v": ev_kvevac, "f5v": ev_kvevac,
        "f4s": eS1, "f5s": eS1,
        "f6v": ev_kvevac, "f7v": e_kr_last,
        "qs_free": [eT1, eT1],
        "qs_sfree": [eS1, eS1],
        "qsc": [0, 0],
        "qc_exp": [0, 0],
        "w_pre": None,
        "last_ffn": False,
        "qsT_free": [eT1, eT1],
        "yT_free": [eT1, eT1],
        "mm2_ev": [eT1, eT1, eT1],
        "lnt1_free": [eS1, eS1],
        "lnt2_free": eS1,
        "r2_ev": [0, 0],
    }

    load_x_tile(0, 0)
    load_x_tile(0, 1)

    eB = dma_in(sB, lambda e: e.dma_start(
        out=bq_r[:], in_=bqd[None, :].broadcast_to([128, D])),
        waits=(eB, (sT, eT1), (sV, eV1), (sS, eS1), (sG, eG1),
               e_scr_all[0], e_scr_all[1], e_ksum_out[1], (sGAM, 16)))
    e_bias_q = eB
    eB = dma_in(sB, lambda e: e.dma_start(
        out=bo_r[:], in_=bod[None, :].broadcast_to([128, D])), waits=(eB,))
    e_bias_o = eB
    eB = dma_in(sB, lambda e: e.dma_start(
        out=b2_r[:], in_=b2d[None, :].broadcast_to([128, D])), waits=(eB,))
    e_bias2 = eB

    def emit_ln_red(buf3, ti, cb, e_prior):
        return ev("vector", sV, lambda e, buf3=buf3, ti=ti, cb=cb:
                  e.tensor_reduce(
            lnt[:, cb + ti:cb + ti + 1], buf3[:, ti, :],
            axis=AX.X, op=OP.add), waits=((sV, e_prior),))

    def emit_ln_seg1b(sq_scale, cb, e_lnt_free):
        silent("vector", lambda e, cb=cb: e.tensor_scalar(
            lnt[:, cb + 2:cb + 4], lnt[:, cb:cb + 2], 1.0 / D, None,
            op0=OP.mult))
        return ev("vector", sV, lambda e, cb=cb, sq_scale=sq_scale:
                  e.tensor_scalar(
            lnt[:, cb + 14:cb + 16], lnt[:, cb + 2:cb + 4], -sq_scale, None,
            op0=OP.mult), waits=((sS, e_lnt_free),))

    def emit_ln_seg2(buf3, ti, sq_scale, cb, dump, e_seg1, extra_waits):
        return ev("scalar", sS, lambda e, buf3=buf3, ti=ti, cb=cb,
                  sq_scale=sq_scale, dump=dump: e.activation(
            dump, buf3[:, ti, :], AF.Square, scale=sq_scale,
            bias=lnt[:, cb + 14 + ti:cb + 15 + ti],
            accum_out=lnt[:, cb + 4 + ti:cb + 5 + ti]),
            waits=((sV, e_seg1),) + extra_waits)

    def emit_ln_seg3(sq_scale, cb, e_seg2):
        e_v1 = ev("vector", sV, lambda e, cb=cb, sq_scale=sq_scale:
                  e.tensor_scalar(
            lnt[:, cb + 4:cb + 6], lnt[:, cb + 4:cb + 6],
            1.0 / (D * sq_scale * sq_scale), EPS_LN,
            op0=OP.mult, op1=OP.add), waits=((sS, e_seg2),))
        silent("scalar", lambda e, cb=cb: e.activation(
            lnt[:, cb + 6:cb + 8], lnt[:, cb + 4:cb + 6], AF.Ln),
            waits=((sV, e_v1),))
        e_ex = ev("scalar", sS, lambda e, cb=cb: e.activation(
            lnt[:, cb + 12:cb + 14], lnt[:, cb + 6:cb + 8], AF.Exp,
            scale=-0.5))
        return ev("vector", sV, lambda e, cb=cb: e.scalar_tensor_tensor(
            lnt[:, cb + 10:cb + 12], lnt[:, cb + 2:cb + 4], -1.0,
            lnt[:, cb + 12:cb + 14], op0=OP.mult, op1=OP.mult),
            waits=((sS, e_ex),))

    def load_xt(si, gi, ti):
        tb = ti % 2
        e_d = dma_in(sXT[tb], lambda e, si=si, gi=gi, ti=ti,
                     tb=tb: e.dma_start(
            out=xt[tb][:], in_=xtm[si, gi * TG + ti, :, :]),
            waits=((sV, st2["xt_free"][tb]),))
        st2["xt_rdy"][tb] = ev("vector", sV, lambda e, tb=tb: e.tensor_tensor(
            xt[tb][:], xt[tb][:], bo_r[:], op=OP.add),
            waits=(e_d, e_bias_o))

    qpb = [(p0, p1), (p2, p3)]

    def emit_qproj(si, gi):
        for ti in range(TG):
            bp = (gi * TG + ti) % 4
            pa, pb_ = qpb[ti]
            e_qp = proj3(bp, wq_h, wq_l, pa, pb_,
                         ((sV, max(xg_state["split"][bp],
                                   st2["r1_evac"][ti])),),
                         set_tfree=True)
            silent("vector", lambda e, ti=ti, pa=pa: e.scalar_tensor_tensor(
                qb[ti][:, 0:512], pa[:, 0:512], 1.0, bq_r[:, 0:512],
                op0=OP.mult, op1=OP.add), waits=((sT, e_qp), e_bias_q))
            st2["qb_evac"][ti] = ev("vector", sV, lambda e, ti=ti, pb_=pb_:
                                    e.scalar_tensor_tensor(
                qb[ti][:, 512:1024], pb_[:, 0:512], 1.0, bq_r[:, 512:1024],
                op0=OP.mult, op1=OP.add))
    def emit_qchain_tile(si, gi, c):
        if c in (2, 3, 4):
            ti, step = 0, c - 2
        elif c in (10, 11, 12):
            ti, step = 1, c - 10
        else:
            return
        if step == 0:
            ev("vector", sV, lambda e, ti=ti: e.tensor_reduce(
                nm[ti][:].rearrange("p (h o) -> p h o", o=1),
                qb[ti][:].rearrange("p (h d) -> p h d", d=HD),
                axis=AX.X, op=OP.max, negate=True),
                waits=((sV, st2["qb_evac"][ti]),))
            e_sub = ev("vector", sV, lambda e, ti=ti: e.tensor_tensor(
                qs[ti][:].rearrange("p (h d) -> p h d", d=HD),
                qb[ti][:].rearrange("p (h d) -> p h d", d=HD),
                nm[ti][:].rearrange("p (h o) -> p h o", o=1).broadcast_to(
                    [128, H, HD]), op=OP.add),
                waits=((sT, st2["qs_free"][ti]), (sS, st2["qs_sfree"][ti])))
            e_exp = ev("scalar", sS, lambda e, ti=ti: e.activation(
                qs[ti][:], qs[ti][:], AF.Exp), waits=((sV, e_sub),))
            st2["qs_sfree"][ti] = e_exp
            st2["qc_exp"][ti] = e_exp
        elif step == 1:
            silent("vector", lambda e, ti=ti: e.tensor_tensor(
                dt[ti][:], qs[ti][:], ksrep[:], op=OP.mult),
                waits=((sS, st2["qc_exp"][ti]), st2["ksrep_dma"]))
            silent("vector", lambda e, ti=ti: e.tensor_reduce(
                den[ti][:].rearrange("p (h o) -> p h o", o=1),
                dt[ti][:].rearrange("p (h d) -> p h d", d=HD),
                axis=AX.X, op=OP.add))
        else:
            silent("vector", lambda e, ti=ti: e.tensor_scalar(
                den[ti][:], den[ti][:], EPS_DEN, None, op0=OP.add))
            silent("vector", lambda e, ti=ti: e.reciprocal(
                den[ti][:], den[ti][:]))
            e_qsc = ev("vector", sV, lambda e, ti=ti: e.tensor_tensor(
                qs[ti][:].rearrange("p (h d) -> p h d", d=HD),
                qs[ti][:].rearrange("p (h d) -> p h d", d=HD),
                den[ti][:].rearrange("p (h o) -> p h o", o=1).broadcast_to(
                    [128, H, HD]), op=OP.mult))
            st2["ksrep_free"] = e_qsc
            st2["qsc"][ti] = e_qsc

    def emit_x1tr(si, gi, tis=(0, 1)):
        par = gi % 2
        x1buf = x1[par]
        for ti in tis:
            for rr in range(2):
                bankT = p6 if rr == 0 else p7
                e_tr = 0
                fvk = "f6v" if rr == 0 else "f7v"
                for j in range(4):
                    c = rr * 4 + j
                    w = ()
                    if j == 0:
                        w = ((sS, st2["x1_ready"][2 * par + ti]),
                             (sV, st2[fvk]))
                    fn = (lambda e, c=c, j=j, x1buf=x1buf, ti=ti, bankT=bankT:
                          e.transpose(bankT[:, j * 128:(j + 1) * 128],
                                      x1buf[:, ti, c * 128:(c + 1) * 128],
                                      ident[:]))
                    if j == 3:
                        e_tr = ev("tensor", sT, fn, waits=w)
                    else:
                        silent("tensor", fn, waits=w)
                e_hi = ev("scalar", sS, lambda e, rr=rr, par=par, ti=ti,
                          bankT=bankT: e.activation(
                    x1Th[par][:, rr * 4:(rr + 1) * 4,
                              ti * 128:(ti + 1) * 128],
                    bankT[:, 0:512], AF.Identity),
                    waits=((sT, max(e_tr, st2["x1T_free"])),))
                e_lo = ev("vector", sV, lambda e, rr=rr, par=par, ti=ti,
                          bankT=bankT: e.scalar_tensor_tensor(
                    x1Tl[par][:, rr * 4:(rr + 1) * 4,
                              ti * 128:(ti + 1) * 128],
                    bankT[:, 0:512], 1.0,
                    x1Th[par][:, rr * 4:(rr + 1) * 4,
                              ti * 128:(ti + 1) * 128],
                    op0=OP.mult, op1=OP.subtract), waits=((sS, e_hi),))
                st2[fvk] = e_lo
                if ti == TG - 1:
                    st2["x1T_rr"][par][rr] = e_lo
                    st2["x1T_hi_rr"][par][rr] = e_hi
                    if rr == 1:
                        st2["x1T_ready"][par] = e_lo

    def emit_ln1_tile(x1buf, ti, par, e_r1):
        sq = 0.5
        ev("vector", sV, lambda e, x1buf=x1buf, ti=ti: e.tensor_reduce(
            lnt[:, ti:ti + 1], x1buf[:, ti, :], axis=AX.X, op=OP.add),
            waits=((sV, e_r1),))
        silent("vector", lambda e, ti=ti: e.tensor_scalar(
            lnt[:, 2 + ti:3 + ti], lnt[:, ti:ti + 1], 1.0 / D, None,
            op0=OP.mult))
        silent("vector", lambda e, ti=ti: e.tensor_scalar(
            lnt[:, 8 + ti:9 + ti], lnt[:, 2 + ti:3 + ti], -1.0, None,
            op0=OP.mult), waits=((sS, st2["lnt1_free"][ti]),))
        e_nmb = ev("vector", sV, lambda e, ti=ti: e.tensor_scalar(
            lnt[:, 14 + ti:15 + ti], lnt[:, 2 + ti:3 + ti], -(D ** -0.5),
            None, op0=OP.mult))
        e_sq = ev("scalar", sS, lambda e, x1buf=x1buf, ti=ti:
                  e.activation(
            qs[ti][:], x1buf[:, ti, :], AF.Square, scale=D ** -0.5,
            bias=lnt[:, 14 + ti:15 + ti],
            accum_out=lnt[:, 4 + ti:5 + ti]),
            waits=((sV, e_nmb), (sT, st2["qs_free"][ti])))
        st2["qs_sfree"][ti] = e_sq
        silent("scalar", lambda e, ti=ti: e.activation(
            lnt[:, 6 + ti:7 + ti], lnt[:, 4 + ti:5 + ti], AF.Ln,
            bias=lnt[:, 32:33]), waits=((sG, e_eps),))
        silent("scalar", lambda e, ti=ti: e.activation(
            lnt[:, 12 + ti:13 + ti], lnt[:, 6 + ti:7 + ti], AF.Exp,
            scale=-0.5))
        silent("scalar", lambda e, ti=ti: e.activation(
            lnt[:, 10 + ti:11 + ti], lnt[:, 8 + ti:9 + ti], AF.Identity,
            scale=lnt[:, 12 + ti:13 + ti]))
        e_n = ev("scalar", sS, lambda e, x1buf=x1buf, ti=ti: e.activation(
            x1buf[:, ti, :], x1buf[:, ti, :], AF.Identity,
            bias=lnt[:, 10 + ti:11 + ti], scale=lnt[:, 12 + ti:13 + ti]))
        st2["lnt1_free"][ti] = e_n
        st2["x1_ready"][2 * par + ti] = e_n

    def emit_attn_rest(si, gi):
        par = gi % 2
        e_yv = [0, 0]
        for ti in range(TG):
            e_qsT_r = [0, 0]
            for rr in range(2):
                bankQ = p4 if rr == 0 else p5
                fv = "f4v" if rr == 0 else "f5v"
                fs = "f4s" if rr == 0 else "f5s"
                e_tr = 0
                for j in range(4):
                    c = rr * 4 + j
                    w = ()
                    if j == 0:
                        w = ((sV, max(st2["qsc"][ti], st2[fv])),
                             (sS, st2[fs]))
                    fn = (lambda e, c=c, j=j, ti=ti, bankQ=bankQ: e.transpose(
                        bankQ[:, j * 128:(j + 1) * 128],
                        qs[ti][:, c * 128:(c + 1) * 128], ident[:]))
                    if j == 3:
                        e_tr = ev("tensor", sT, fn, waits=w)
                    else:
                        silent("tensor", fn, waits=w)
                if rr == 1:
                    st2["qs_free"][ti] = e_tr
                e_qsT_r[rr] = ev("scalar", sS, lambda e, rr=rr, ti=ti,
                                 bankQ=bankQ: e.activation(
                    qsT[ti][:, rr * 4:(rr + 1) * 4, :].rearrange(
                        "p a b -> p (a b)"), bankQ[:, 0:512], AF.Identity),
                    waits=((sT, max(e_tr, st2["qsT_free"][ti])),))
                st2[fs] = e_qsT_r[rr]
            e_yev_s = 0
            e_yev_v = 0
            for rr in range(2):
                bankY = p6 if rr == 0 else p7
                fv = "f6v" if rr == 0 else "f7v"
                e_ymm = 0
                for j in range(4):
                    pr = rr * 4 + j
                    w = ()
                    if j == 0:
                        w = ((sV, st2[fv]), (sG, e_kvz[si]),
                             (sS, e_qsT_r[rr]))
                    fn = (lambda e, pr=pr, j=j, ti=ti, si=si, bankY=bankY:
                          e.matmul(
                        bankY[:, j * 128:(j + 1) * 128],
                        kvacc[si][:, pr * 129:pr * 129 + 128],
                        qsT[ti][:, pr, :], start=True, stop=True))
                    if j == 3:
                        e_ymm = ev("tensor", sT, fn, waits=w)
                    else:
                        silent("tensor", fn, waits=w)
                if rr == 1:
                    st2["qsT_free"][ti] = e_ymm
                e_yev_s = ev("scalar", sS, lambda e, rr=rr, ti=ti, bankY=bankY:
                             e.activation(
                    yTh[ti][:, rr * 4:(rr + 1) * 4, :].rearrange(
                        "p a b -> p (a b)"), bankY[:, 0:512], AF.Identity),
                    waits=((sT, max(e_ymm, st2["yT_free"][ti])), (sS, e_yev_s)))
                e_yev_v = ev("vector", sV, lambda e, rr=rr, ti=ti, bankY=bankY:
                             e.scalar_tensor_tensor(
                    yTl[ti][:, rr * 4:(rr + 1) * 4, :].rearrange(
                        "p a b -> p (a b)"),
                    bankY[:, 0:512], 1.0,
                    yTh[ti][:, rr * 4:(rr + 1) * 4, :].rearrange(
                        "p a b -> p (a b)"),
                    op0=OP.mult, op1=OP.subtract), waits=((sS, e_yev_s),))
                st2[fv] = e_yev_v
            e_yv[ti] = e_yev_v
        x1buf = x1[par]
        e_r1 = 0
        for ti in range(TG):
            pa, pb_ = qpb[ti]
            e_op = 0
            for c in range(NP):
                w = (((sV, max(st2["qb_evac"][ti], e_yv[ti],
                               st2["h_evac"][0], st2["h_evac"][1])),
                      (sS, max(st2["h_hi"][0], st2["h_hi"][1])))
                     if c == 0 else ())
                st = (c == 0)
                sp = (c == NP - 1)
                silent("tensor", lambda e, c=c, st=st, ti=ti, pa=pa: e.matmul(
                    pa[:, 0:512], yTh[ti][:, c, :], wo_h[:, c, 0:512],
                    start=st, stop=False), waits=w)
                silent("tensor", lambda e, c=c, st=st, ti=ti, pb_=pb_: e.matmul(
                    pb_[:, 0:512], yTh[ti][:, c, :], wo_h[:, c, 512:1024],
                    start=st, stop=False))
                silent("tensor", lambda e, c=c, ti=ti, pa=pa: e.matmul(
                    pa[:, 0:512], yTh[ti][:, c, :], wo_l[:, c, 0:512],
                    start=False, stop=False))
                silent("tensor", lambda e, c=c, ti=ti, pb_=pb_: e.matmul(
                    pb_[:, 0:512], yTh[ti][:, c, :], wo_l[:, c, 512:1024],
                    start=False, stop=False))
                silent("tensor", lambda e, c=c, sp=sp, ti=ti, pa=pa: e.matmul(
                    pa[:, 0:512], yTl[ti][:, c, :], wo_h[:, c, 0:512],
                    start=False, stop=sp))
                if c == NP - 1:
                    e_op = ev("tensor", sT, lambda e, c=c, ti=ti, pb_=pb_:
                              e.matmul(pb_[:, 0:512], yTl[ti][:, c, :],
                                       wo_h[:, c, 512:1024],
                                       start=False, stop=True))
                else:
                    silent("tensor", lambda e, c=c, ti=ti, pb_=pb_: e.matmul(
                        pb_[:, 0:512], yTl[ti][:, c, :], wo_h[:, c, 512:1024],
                        start=False, stop=False))
            st2["yT_free"][ti] = e_op
            tb = ti % 2
            silent("vector", lambda e, pa=pa, x1buf=x1buf, ti=ti, tb=tb:
                   e.scalar_tensor_tensor(
                x1buf[:, ti, 0:512], pa[:, 0:512], 1.0, xt[tb][:, 0:512],
                op0=OP.mult, op1=OP.add),
                waits=((sT, e_op), (sV, st2["xt_rdy"][tb])))
            e_r1 = ev("vector", sV, lambda e, pb_=pb_, x1buf=x1buf, ti=ti,
                      tb=tb: e.scalar_tensor_tensor(
                x1buf[:, ti, 512:1024], pb_[:, 0:512], 1.0,
                xt[tb][:, 512:1024], op0=OP.mult, op1=OP.add))
            st2["r1_evac"][ti] = e_r1
            st2["xt_free"][tb] = e_r1
            emit_ln1_tile(x1buf, ti, par, e_r1)

    def emit_mm2(c, cb, wb, e_w2c, e_hev, is_first, is_last):
        e_mm2 = 0
        w0 = ((sV, max(e_hev,
                       st2["ffnout_evac"] if is_first else 0,
                       max(st2["f4v"], st2["f5v"], st2["f6v"],
                           st2["f7v"]) if is_first else 0)),
              ((sS, max(st2["f4s"], st2["f5s"])) if is_first else (sS, 0)),
              e_w2c)
        emitted = False
        for ti in range(TG):
            for hv in range(2):
                hbuf = hh[cb] if hv == 0 else hl[cb]
                for nci in range(2):
                    stf = (c == 0 and hv == 0)
                    spf = (c == F4C - 1 and hv == 1)
                    w = w0 if not emitted else ()
                    emitted = True
                    fn = (lambda e, wb=wb, ti=ti, hbuf=hbuf, nci=nci,
                          stf=stf, spf=spf: e.matmul(
                        PB[4 + 2 * ti + nci][:, 0:512],
                        hbuf[:, ti * 128:(ti + 1) * 128],
                        w2s[wb][:, nci * 512:(nci + 1) * 512],
                        start=stf, stop=spf))
                    if ti == TG - 1 and hv == 1 and nci == 1:
                        e_mm2 = ev("tensor", sT, fn, waits=w)
                    else:
                        silent("tensor", fn, waits=w)
        st2["mm2_ev"][wb] = e_mm2
        return e_mm2

    def emit_ffn(si, gi, inject=None):
        par = gi % 2
        e_mm1 = 0
        pend_mm2 = None
        e_w2c = [0, 0, 0]
        for c in range(F4C):
            if inject is not None:
                inject(c)
            if c == 16:
                for ti in range(TG):
                    silent("vector", lambda e, ti=ti, par=par: e.tensor_tensor(
                        x1[par][:, ti, :], x1[par][:, ti, :], b2_r[:],
                        op=OP.add),
                        waits=((sV, st2["x1T_ready"][par]), e_bias2))
            cb = c % 2
            wb = c % 3
            if c < 3 and st2["w_pre"] is not None:
                e_w1, e_w2c[wb] = st2["w_pre"][c]
            else:
                e_w1 = dma_in(sW1[wb], lambda e, c=c, wb=wb: e.dma_start(
                    out=w1s[wb][:], in_=w1q[c, :, :, :]),
                    waits=(e_scr_all[0], e_scr_all[1],
                           (sT, st2["w1_free"][wb])))
                e_w2c[wb] = dma_in(sW2[wb], lambda e, c=c, wb=wb: e.dma_start(
                    out=w2s[wb][:], in_=w2q[c, :, :]),
                    waits=((sT, st2["mm2_ev"][wb]),))
            hp = PB[cb]
            for fc in range(NP):
                w = ()
                if fc == 0:
                    w = (e_w1,
                         (sV, max(st2["h_evac"][cb],
                                  st2["qb_evac"][0] if c < 2 else 0)),
                         (sS, max(st2["h_hi"][cb],
                                  st2["x1T_hi_rr"][par][0])))
                elif fc == 4:
                    w = ((sS, st2["x1T_hi_rr"][par][1]),)
                silent("tensor", lambda e, c=c, fc=fc, wb=wb, par=par, hp=hp:
                       e.matmul(hp[:, 0:TG * 128], w1s[wb][:, fc, :],
                                x1Th[par][:, fc, :],
                                start=(fc == 0), stop=False), waits=w)
            for fc in range(NP):
                w = ()
                if fc == 0:
                    w = ((sV, st2["x1T_rr"][par][0]),)
                elif fc == 4:
                    w = ((sV, st2["x1T_rr"][par][1]),)
                if fc == NP - 1:
                    e_mm1 = ev("tensor", sT, lambda e, c=c, fc=fc, wb=wb,
                               par=par, hp=hp: e.matmul(
                        hp[:, 0:TG * 128], w1s[wb][:, fc, :],
                        x1Tl[par][:, fc, :], start=False, stop=True), waits=w)
                else:
                    silent("tensor", lambda e, c=c, fc=fc, wb=wb, par=par,
                           hp=hp: e.matmul(
                        hp[:, 0:TG * 128], w1s[wb][:, fc, :],
                        x1Tl[par][:, fc, :], start=False, stop=False),
                        waits=w)
            st2["w1_free"][wb] = e_mm1
            e_hi = ev("scalar", sS, lambda e, c=c, cb=cb, hp=hp: e.activation(
                hh[cb][:], hp[:, 0:TG * 128], AF.Relu, bias=b1_c[:, c:c + 1]),
                waits=((sT, e_mm1), e_b1c))
            e_h32 = ev("scalar", sS, lambda e, c=c, cb=cb, hp=hp: e.activation(
                h32[cb][:], hp[:, 0:TG * 128], AF.Relu, bias=b1_c[:, c:c + 1]))
            st2["h_hi"][cb] = e_h32
            st2["h_evac"][cb] = ev("vector", sV, lambda e, cb=cb:
                                   e.scalar_tensor_tensor(
                hl[cb][:], h32[cb][:], 1.0, hh[cb][:],
                op0=OP.mult, op1=OP.subtract), waits=((sS, e_h32),))
            if pend_mm2 is not None:
                emit_mm2(*pend_mm2)
            pend_mm2 = (c, cb, wb, e_w2c[wb], st2["h_evac"][cb], c == 0,
                        c == F4C - 1)
        e_mm2 = emit_mm2(*pend_mm2)
        e_r2_t = [0, 0]
        for ti in range(TG):
            x1buf = x1[par]
            silent("vector", lambda e, ti=ti, x1buf=x1buf:
                   e.scalar_tensor_tensor(
                ot2[:, ti, 0:512], PB[4 + 2 * ti][:, 0:512], 1.0,
                x1buf[:, ti, 0:512], op0=OP.mult, op1=OP.add),
                waits=((sT, e_mm2), st2["ot_free"][ti % 2]))
            e_r2 = ev("vector", sV, lambda e, ti=ti, x1buf=x1buf:
                      e.scalar_tensor_tensor(
                ot2[:, ti, 512:1024], PB[5 + 2 * ti][:, 0:512], 1.0,
                x1buf[:, ti, 512:1024], op0=OP.mult, op1=OP.add))
            e_r2_t[ti] = e_r2
        st2["ffnout_evac"] = e_r2_t[1]
        if not st2["last_ffn"]:
            pre = []
            for c in range(3):
                wb2 = c % 3
                ew1 = dma_out(sWP1[wb2], lambda e, c=c, wb2=wb2: e.dma_start(
                    out=w1s[wb2][:], in_=w1q[c, :, :, :]),
                    waits=((sT, st2["w1_free"][wb2]),))
                ew2 = dma_out(sWP2[wb2], lambda e, c=c, wb2=wb2: e.dma_start(
                    out=w2s[wb2][:], in_=w2q[c, :, :]),
                    waits=((sT, st2["mm2_ev"][wb2]),))
                pre.append((ew1, ew2))
            st2["w_pre"] = pre
        st2["f4v"] = max(st2["f4v"], e_r2_t[0])
        st2["f5v"] = max(st2["f5v"], e_r2_t[0])
        st2["f6v"] = max(st2["f6v"], e_r2_t[1])
        st2["f7v"] = max(st2["f7v"], e_r2_t[1])
        st2["r2_ev"] = e_r2_t

    LN2_SQ = 1.0 / 512

    def emit_ln2_seg4_store(psi, pgi, e_s3):
        for ti in range(TG):
            ob = ti % 2
            e_o = ev("scalar", sS, lambda e, ti=ti: e.activation(
                ot2[:, ti, :], ot2[:, ti, :], AF.Identity,
                bias=lnt[:, 26 + ti:27 + ti], scale=lnt[:, 28 + ti:29 + ti]),
                waits=((sV, e_s3),))
            st2["lnt2_free"] = e_o
            st2["ot_free"][ob] = dma_out(sOT[ob], lambda e, psi=psi, pgi=pgi,
                                         ti=ti: e.dma_start(
                out=out[psi, pgi * TG + ti, :, :], in_=ot2[:, ti, :]),
                waits=((sS, e_o),))

    def mk_ln2_injector(psi, pgi, e_r2_pair):
        state = {}

        def inj(c):
            if c == 6:
                emit_ln_red(ot2, 0, 16, e_r2_pair[0])
                emit_ln_red(ot2, 1, 16, e_r2_pair[1])
                state["s1"] = emit_ln_seg1b(LN2_SQ, 16, st2["lnt2_free"])
            elif c == 7:
                e_d = 0
                for ti in range(TG):
                    e_d = emit_ln_seg2(ot2, ti, LN2_SQ, 16, qs[1][:],
                                       state["s1"],
                                       ((sT, st2["qs_free"][1]),))
                st2["qs_sfree"][1] = e_d
                state["s2"] = e_d
            elif c == 9:
                state["s3"] = emit_ln_seg3(LN2_SQ, 16, state["s2"])
            elif c == 14:
                emit_ln2_seg4_store(psi, pgi, state["s3"])
        return inj

    pend = None
    pprev = None
    pprev_r2 = (0, 0)
    groups = [(si, gi) for si in range(NSEQ) for gi in range(p2_groups)]
    for idx, (si, gi) in enumerate(groups):
        if gi == 0:
            st2["ksrep_dma"] = dma_in(sKS, lambda e, si=si: e.dma_start(
                out=ksrep[:],
                in_=ksum_hbm[si, :][None, :].broadcast_to([128, D])),
                waits=(e_ksum_out[si], (sV, st2["ksrep_free"])))
        if idx == 0:
            load_xt(si, gi, 0)
            load_xt(si, gi, 1)
        emit_qproj(si, gi)
        if idx == 0:
            emit_seq_boundary(1)
            st2["f4v"] = ev_kvevac
            st2["f5v"] = ev_kvevac
            st2["f6v"] = ev_kvevac
            st2["f7v"] = e_kr_last
        if pend is not None:
            emit_x1tr(pend[0], pend[1], tis=(1,))
            inj2 = (mk_ln2_injector(pprev[0], pprev[1], pprev_r2)
                    if pprev is not None else None)

            def inject(c, si=si, gi=gi, inj2=inj2):
                emit_qchain_tile(si, gi, c)
                if inj2 is not None:
                    inj2(c)
            emit_ffn(pend[0], pend[1], inject=inject)
            pprev = pend
            pprev_r2 = tuple(st2["r2_ev"])
        else:
            for cc in (2, 3, 4, 10, 11, 12):
                emit_qchain_tile(si, gi, cc)
        if idx + 1 < len(groups):
            nsi, ngi = groups[idx + 1]
            load_x_tile(nsi, ngi * TG + 0)
            load_x_tile(nsi, ngi * TG + 1)
        emit_attn_rest(si, gi)
        if idx + 1 < len(groups):
            nsi, ngi = groups[idx + 1]
            load_xt(nsi, ngi, 0)
            load_xt(nsi, ngi, 1)
        emit_x1tr(si, gi, tis=(0,))
        pend = (si, gi)
    if pend is not None:
        emit_x1tr(pend[0], pend[1], tis=(1,))
        inj2 = (mk_ln2_injector(pprev[0], pprev[1], pprev_r2)
                if pprev is not None else None)
        st2["last_ffn"] = True
        emit_ffn(pend[0], pend[1],
                 inject=(lambda c, inj2=inj2: inj2(c)) if inj2 else None)
        emit_ln_red(ot2, 0, 16, st2["r2_ev"][0])
        emit_ln_red(ot2, 1, 16, st2["r2_ev"][1])
        e_s1 = emit_ln_seg1b(LN2_SQ, 16, st2["lnt2_free"])
        e_d = 0
        for ti in range(TG):
            e_d = emit_ln_seg2(ot2, ti, LN2_SQ, 16, qs[1][:], e_s1,
                               ((sT, st2["qs_free"][1]),))
        st2["qs_sfree"][1] = e_d
        e_s3 = emit_ln_seg3(LN2_SQ, 16, e_d)
        emit_ln2_seg4_store(pend[0], pend[1], e_s3)
    silent("gpsimd", lambda e: e.nop(),
           waits=((sOT[0], sOT[0].n), (sOT[1], sOT[1].n),
                  (sQW[0], sQW[0].n), (sQW[1], sQW[1].n),
                  (sKSO, sKSO.n), (sGAM, sGAM.n)))

    with nc.Block() as block:
        @block.sync
        def _(e):
            sch.replay("sync", e)

        @block.tensor
        def _(e):
            sch.replay("tensor", e)

        @block.vector
        def _(e):
            sch.replay("vector", e)

        @block.scalar
        def _(e):
            sch.replay("scalar", e)

        @block.gpsimd
        def _(e):
            sch.replay("gpsimd", e)

    for g in reversed(alloc):
        g.__exit__(None, None, None)
    return nc


def stage_inputs(L0_attn_x, L0_attn_Wq, L0_attn_bq, L0_attn_Wk, L0_attn_bk,
                 L0_attn_Wv, L0_attn_bv, L0_attn_Wo, L0_attn_bo,
                 L0_ffn_W1, L0_ffn_b1, L0_ffn_W2, L0_ffn_b2):
    x = np.asarray(L0_attn_x, dtype=np.float32)
    shared = {
        "wqT": np.ascontiguousarray(np.asarray(L0_attn_Wq).T).reshape(NP, 128, D),
        "wkT": np.ascontiguousarray(np.asarray(L0_attn_Wk).T).reshape(NP, 128, D),
        "wvT": np.ascontiguousarray(np.asarray(L0_attn_Wv).T).reshape(NP, 128, D),
        "woT": np.ascontiguousarray(np.asarray(L0_attn_Wo).T).reshape(NP, 128, D),
        "w1r": np.ascontiguousarray(
            np.asarray(L0_ffn_W1).T.reshape(NP, 128, F4C, 128).transpose(
                2, 1, 0, 3)),
        "w2r": np.ascontiguousarray(np.asarray(L0_ffn_W2).T).reshape(
            F4C, 128, D),
        "bqd": np.ascontiguousarray(L0_attn_bq),
        "bkd": np.ascontiguousarray(L0_attn_bk),
        "bvd": np.ascontiguousarray(L0_attn_bv),
        "bod": np.ascontiguousarray(L0_attn_bo),
        "b1d": np.ascontiguousarray(np.asarray(L0_ffn_b1).reshape(F4C, 128).T),
        "b2d": np.ascontiguousarray(L0_ffn_b2),
    }
    in_maps = []
    for c in range(NCORES):
        xs = x[2 * c:2 * c + 2]
        m = dict(shared)
        m["xtm"] = np.ascontiguousarray(xs).reshape(NSEQ, NT, 128, D)
        m["xfm"] = np.ascontiguousarray(
            xs.reshape(NSEQ, NT, 128, NP, 128).transpose(0, 1, 4, 3, 2)
        ).reshape(NSEQ, NT, 128, NP * 128)
        in_maps.append(m)
    return in_maps


def kernel(**inputs):
    in_maps = stage_inputs(**inputs)
    nc = build_program()
    res = run_bass_kernel_spmd(nc, in_maps, core_ids=list(range(NCORES)))
    parts = [np.asarray(r["out"]).reshape(NSEQ * S, D) for r in res.results]
    return np.concatenate(parts, axis=0).reshape(16, S, D)
